# version 06
# baseline (speedup 1.0000x reference)
import numpy as np
from contextlib import ExitStack

import concourse.bass as bass
import concourse.mybir as mybir
from concourse.bass_utils import run_bass_kernel_spmd

F32 = mybir.dt.float32
BF16 = mybir.dt.bfloat16
AF = mybir.ActivationFunctionType
ALU = mybir.AluOpType
AX = mybir.AxisListType

D = 1024
S = 8192
B = 2
NCORE = 8
NT = 16
CH = 8
DC = 8
C_SHIFT = 16.0
NEXP = 8
D_HID = 4096

CHUNK_OF_SLOT = [7 - s // 2 for s in range(NT)]


def _assignment(core):
    k = core % 4
    return [8 * CHUNK_OF_SLOT[s] + 2 * k + (s % 2) for s in range(NT)]


def _kv_perm_tiles(core):
    k = core % 4
    tiles = []
    for c in range(CH):
        own = [8 * c + 2 * k, 8 * c + 2 * k + 1]
        rest = [8 * c + j for j in range(8) if 8 * c + j not in own]
        tiles.extend(own + rest)
    return tiles


class Ctr:
    def __init__(self, ctx, nc, name):
        self.sem = ctx.enter_context(nc.semaphore(name))
        self.n = 0

    def inc(self, inst, k=1):
        inst.then_inc(self.sem, k)
        self.n += k
        return inst

    def wait(self, eng, val=None):
        v = self.n if val is None else val
        if v > 0:
            eng.wait_ge(self.sem, v)


def build(nc):
    ctx = ExitStack()

    def din(name, shape, dtype=F32):
        return nc.dram_tensor(name, shape, dtype, kind="ExternalInput").ap()

    xkv_cm = din("xkv_cm", [D, S])
    xown = din("xown", [NT * 128, D])
    amask = din("amask", [CH, 1024, 256])
    Wq = din("Wq", [D, D])
    Wk = din("Wk", [D, D])
    Wv = din("Wv", [D, D])
    Wo = din("Wo", [D, D])
    bq_pt = din("bq_pt", [128, 8])
    bk_pt = din("bk_pt", [128, 8])
    bv_row = din("bv_row", [1, D])
    bo_row = din("bo_row", [1, D])
    ident_in = din("ident_bf", [128, 128], BF16)
    Wg = din("Wg", [D, 8])
    bg_bc = din("bg_bc", [128, 8])
    W0 = din("W0", [8, D, 4096])
    W1 = din("W1", [8, D, 4096])
    W2 = din("W2", [8, 4096, D])
    b0_pt = din("b0_pt", [128, 256])
    b1_pt = din("b1_pt", [128, 256])
    b2_in = din("b2_in", [8, D])
    Wattn = din("Wattn", [D, 3 * D])
    battn_pt = din("battn_pt", [128, 16])
    battnv_row = din("battnv_row", [1, D])
    Wproj = din("Wproj", [D, D])
    bproj_row = din("bproj_row", [1, D])
    lcmask4 = din("lcmask4", [128, 512])
    bmask_in = din("bmask_in", [128, 64])

    out = nc.dram_tensor("out", [NT * 128, D], F32, kind="ExternalOutput").ap()

    k_scr = nc.dram_tensor("k_scr", [CH, 128, DC * 1024], BF16, kind="Internal").ap()
    v_scr = nc.dram_tensor("v_scr", [CH, 128, 8 * 1024], BF16, kind="Internal").ap()
    q_scr = nc.dram_tensor("q_scr", [128, DC, 2048], BF16, kind="Internal").ap()

    E = ctx.enter_context
    ident = E(nc.sbuf_tensor("ident", [128, 128], BF16))
    ones_col = E(nc.sbuf_tensor("ones_col", [128, 1], BF16))
    ones_row_f = E(nc.sbuf_tensor("ones_row_f", [1, 128], F32))
    ones_row_b = E(nc.sbuf_tensor("ones_row_b", [1, 128], BF16))
    negC = E(nc.sbuf_tensor("negC", [128, 1], F32))
    inv32 = E(nc.sbuf_tensor("inv32", [128, 1], F32))
    epsc = E(nc.sbuf_tensor("epsc", [128, 1], F32))
    eps1 = E(nc.sbuf_tensor("eps1", [1, 1], F32))

    block = E(nc.Block())

    c_ld = Ctr(ctx, nc, "c_ld")
    c_lw = Ctr(ctx, nc, "c_lw")
    c_vx = Ctr(ctx, nc, "c_vx")
    c_sx = Ctr(ctx, nc, "c_sx")
    c_tn = Ctr(ctx, nc, "c_tn")
    c_st = Ctr(ctx, nc, "c_st")
    c_bar = Ctr(ctx, nc, "c_bar")
    s_x = [Ctr(ctx, nc, f"s_x{i}") for i in range(2)]
    s_kv = [Ctr(ctx, nc, f"s_kv{i}") for i in range(2)]
    s_qp = Ctr(ctx, nc, "s_qp")
    s_am = Ctr(ctx, nc, "s_am")
    dma_ctrs = [c_st, c_ld, c_lw, s_qp, s_am] + s_x + s_kv

    ENGS = ["sync", "gpsimd", "tensor", "vector", "scalar"]

    def barrier():
        def q(sync):
            for ctr in dma_ctrs:
                ctr.wait(sync)
        block.sync(q)
        target = c_bar.n + len(ENGS)
        for name in ENGS:
            def f(eng, target=target):
                eng.sem_inc(c_bar.sem, 1)
                eng.wait_ge(c_bar.sem, target)
            getattr(block, name)(f)
        c_bar.n = target

    def init_vec(vector):
        vector.memset(ones_col[:], 1.0)
        vector.memset(ones_row_f[:], 1.0)
        vector.memset(ones_row_b[:], 1.0)
        vector.memset(negC[:], -C_SHIFT)
        vector.memset(inv32[:], 1.0 / 32.0)
        vector.memset(epsc[:], 1e-5)
        c_vx.inc(vector.memset(eps1[:], 1e-5))
    block.vector(init_vec)

    def init_sync(sync):
        c_ld.inc(sync.dma_start(out=ident[:], in_=ident_in), 16)
    block.sync(init_sync)
    ld_init = c_ld.n

    ctxA0 = ExitStack()
    EA = ctxA0.enter_context
    xstg = [EA(nc.sbuf_tensor(f"xstg{b}", [128, DC * 512], F32)) for b in range(2)]
    xlnb = [EA(nc.sbuf_tensor(f"xlnb{b}", [128, DC * 512], BF16)) for b in range(2)]
    xbb = EA(nc.sbuf_tensor("xbb", [128, 2 * 512], BF16))
    x2b = EA(nc.sbuf_tensor("x2b", [128, 2 * 512], BF16))
    tmpf = EA(nc.sbuf_tensor("tmpf", [128, 512], F32))
    mean_r = EA(nc.sbuf_tensor("mean_r", [1, 512], F32))
    t_r = EA(nc.sbuf_tensor("t_r", [1, 512], F32))
    msq_r = EA(nc.sbuf_tensor("msq_r", [1, 512], F32))
    var_r = EA(nc.sbuf_tensor("var_r", [1, 512], F32))
    std_r = EA(nc.sbuf_tensor("std_r", [1, 512], F32))
    rstd_r = EA(nc.sbuf_tensor("rstd_r", [1, 512], F32))
    Wq_sb = EA(nc.sbuf_tensor("Wq_sb", [128, DC * 1024], BF16))
    Wk_sb = EA(nc.sbuf_tensor("Wk_sb", [128, DC * 1024], BF16))
    Wv_sb = EA(nc.sbuf_tensor("Wv_sb", [128, DC * 1024], BF16))
    bqp = EA(nc.sbuf_tensor("bqp", [128, 8], F32))
    bkp = EA(nc.sbuf_tensor("bkp", [128, 8], F32))
    bvr = EA(nc.sbuf_tensor("bvr", [1, D], BF16))
    K_sb = EA(nc.sbuf_tensor("K_sb", [128, DC * 1024], BF16))
    V_sb = EA(nc.sbuf_tensor("V_sb", [128, 8 * 1024], BF16))
    Qc_sb = EA(nc.sbuf_tensor("Qc_sb", [128, DC * 256], BF16))

    pst1 = EA(nc.psum_tensor("pst1", [1, 512], F32))
    pst2 = EA(nc.psum_tensor("pst2", [1, 512], F32))
    pbcm = EA(nc.psum_tensor("pbcm", [128, 512], F32))
    pbcr = EA(nc.psum_tensor("pbcr", [128, 512], F32))
    pmm = [EA(nc.psum_tensor(f"pmmA{i}", [128, 512], F32)) for i in range(4)]

    def a0_weights(gpsimd):
        for W, sb in ((Wq, Wq_sb), (Wk, Wk_sb), (Wv, Wv_sb)):
            for kc in range(DC):
                c_lw.inc(gpsimd.dma_start(
                    out=sb[:, kc * 1024:(kc + 1) * 1024],
                    in_=W[kc * 128:(kc + 1) * 128, :]), 16)
        c_lw.inc(gpsimd.dma_start(out=bvr[:], in_=bv_row), 16)
    block.gpsimd(a0_weights)
    lw_a0 = c_lw.n

    def a0_bias(sync):
        c_ld.inc(sync.dma_start(out=bqp[:], in_=bq_pt), 16)
        c_ld.inc(sync.dma_start(out=bkp[:], in_=bk_pt), 16)
    block.sync(a0_bias)
    ld_a0 = c_ld.n

    xstg_free = [(0, 0)] * 2
    xlnb_free = [0] * 2
    xbb_free = [0] * 2
    x2b_free = [0] * 2
    pst_free = [0]
    pmm_free = [("sx", 0)] * 4
    sx_evac = 0
    st_kv = 0

    def a0_prep(c, h, sb):
        global_state = None

        def stage(sync):
            vfree, sfree = xstg_free[sb]
            c_vx.wait(sync, vfree)
            c_sx.wait(sync, sfree)
            src = xkv_cm[:, 1024 * c + 512 * h: 1024 * c + 512 * (h + 1)]
            src = src.rearrange("(dc p) n -> p dc n", p=128)
            s_x[sb].inc(sync.dma_start(
                out=xstg[sb].rearrange("p (dc n) -> p dc n", dc=DC), in_=src), 16)
        block.sync(stage)
        ld_here = s_x[sb].n

        vx_cast = []
        sx_sq = []
        for dc in range(DC):
            def cast1(vector, dc=dc):
                if dc == 0:
                    s_x[sb].wait(vector, ld_here)
                c_tn.wait(vector, xbb_free[dc % 2])
                c_vx.inc(vector.tensor_copy(
                    out=xbb[:, (dc % 2) * 512:(dc % 2 + 1) * 512],
                    in_=xstg[sb][:, dc * 512:(dc + 1) * 512]))
                vx_cast.append(c_vx.n)

            def sq1(vector, dc=dc):
                if dc == 0:
                    s_x[sb].wait(vector, ld_here)
                c_tn.wait(vector, x2b_free[dc % 2])
                c_vx.inc(vector.tensor_mul(
                    out=x2b[:, (dc % 2) * 512:(dc % 2 + 1) * 512],
                    in0=xstg[sb][:, dc * 512:(dc + 1) * 512],
                    in1=xstg[sb][:, dc * 512:(dc + 1) * 512]))
                sx_sq.append(c_vx.n)

            def mm1(tensor, dc=dc):
                if dc == 0:
                    c_sx.wait(tensor, pst_free[0])
                c_vx.wait(tensor, vx_cast[dc])
                ins = tensor.matmul(out=pst1[:], lhsT=ones_col[:],
                                    rhs=xbb[:, (dc % 2) * 512:(dc % 2 + 1) * 512],
                                    start=(dc == 0), stop=(dc == DC - 1),
                                    skip_group_check=True)
                c_tn.inc(ins)
                xbb_free[dc % 2] = c_tn.n
                c_vx.wait(tensor, sx_sq[dc])
                ins = tensor.matmul(out=pst2[:], lhsT=ones_col[:],
                                    rhs=x2b[:, (dc % 2) * 512:(dc % 2 + 1) * 512],
                                    start=(dc == 0), stop=(dc == DC - 1),
                                    skip_group_check=True)
                c_tn.inc(ins)
                x2b_free[dc % 2] = c_tn.n
            block.vector(cast1)
            block.vector(sq1)
            block.tensor(mm1)
        tn_stats = c_tn.n

        def rowstats_s(scalar):
            c_tn.wait(scalar, tn_stats)
            c_sx.inc(scalar.mul(out=mean_r[:], in_=pst1[:], mul=1.0 / D))
            scalar.mul(out=t_r[:], in_=pst2[:], mul=1.0 / D)
            c_sx.wait(scalar)
            c_sx.inc(scalar.activation(out=msq_r[:], in_=mean_r[:], func=AF.Square))
        block.scalar(rowstats_s)
        pst_free[0] = c_sx.n
        sx_rows = c_sx.n

        def rowstats_v(vector):
            c_sx.wait(vector, sx_rows)
            c_vx.inc(vector.tensor_tensor(out=var_r[:], in0=t_r[:], in1=msq_r[:],
                                          op=ALU.subtract))
        block.vector(rowstats_v)
        vx_var = c_vx.n

        def rowstats_s2(scalar):
            c_vx.wait(scalar, vx_var)
            c_sx.inc(scalar.activation(out=std_r[:], in_=var_r[:], func=AF.Sqrt,
                                       bias=eps1[:]))
        block.scalar(rowstats_s2)
        sx_std = c_sx.n

        def rowstats_v2(vector):
            c_sx.wait(vector, sx_std)
            c_vx.inc(vector.reciprocal(out=rstd_r[:], in_=std_r[:]))
        block.vector(rowstats_v2)
        vx_rstd = c_vx.n

        def bcast(tensor):
            c_vx.wait(tensor, vx_rstd)
            tensor.matmul(out=pbcm[:], lhsT=ones_row_f[:], rhs=mean_r[:],
                          start=True, stop=True, skip_group_check=True)
            c_tn.inc(tensor.matmul(out=pbcr[:], lhsT=ones_row_f[:], rhs=rstd_r[:],
                                   start=True, stop=True, skip_group_check=True))
        block.tensor(bcast)
        tn_bc = c_tn.n

        def apply_ln(vector):
            c_tn.wait(vector, tn_bc)
            c_tn.wait(vector, xlnb_free[sb])
            for dc in range(DC):
                if dc > 0:
                    c_vx.wait(vector)
                c_vx.inc(vector.tensor_tensor(
                    out=tmpf[:],
                    in0=xstg[sb][:, dc * 512:(dc + 1) * 512],
                    in1=pbcm[:], op=ALU.subtract))
                c_vx.wait(vector)
                c_vx.inc(vector.tensor_tensor(
                    out=xlnb[sb][:, dc * 512:(dc + 1) * 512],
                    in0=tmpf[:], in1=pbcr[:], op=ALU.mult))
        block.vector(apply_ln)
        xstg_free[sb] = (c_vx.n, 0)
        return c_vx.n

    def a0_proj(c, h, sb, vx_ln):
        jobs = [("k", dk) for dk in range(DC)]
        jobs += [("v", (tt, vh)) for tt in range(4) for vh in range(2)]
        if h == 0:
            jobs += [("q", dq) for dq in range(DC)]

        first_proj = True
        for kind, idx in jobs:
            def proj(tensor, kind=kind, idx=idx, first=first_proj):
                if first:
                    c_vx.wait(tensor, vx_ln)
                    c_lw.wait(tensor, lw_a0)
                    c_ld.wait(tensor, ld_a0)
                    c_st.wait(tensor, st_state[0])
                if kind == "k":
                    dk = idx
                    bank = dk % 2
                    (c_vx if pmm_free[bank][0] == "vx" else c_sx).wait(tensor, pmm_free[bank][1])
                    for kc in range(DC):
                        ins = tensor.matmul(
                            out=pmm[bank][:],
                            lhsT=Wk_sb[:, kc * 1024 + dk * 128: kc * 1024 + dk * 128 + 128],
                            rhs=xlnb[sb][:, kc * 512:(kc + 1) * 512],
                            start=(kc == 0), stop=(kc == DC - 1), skip_group_check=True)
                    c_tn.inc(ins)
                elif kind == "v":
                    tt, vh = idx
                    bank = 2 + (tt * 2 + vh) % 2
                    (c_vx if pmm_free[bank][0] == "vx" else c_sx).wait(tensor, pmm_free[bank][1])
                    for kc in range(DC):
                        tensor.matmul(
                            out=pmm[bank][:],
                            lhsT=xlnb[sb][:, kc * 512 + tt * 128: kc * 512 + tt * 128 + 128],
                            rhs=Wv_sb[:, kc * 1024 + vh * 512: kc * 1024 + (vh + 1) * 512],
                            start=(kc == 0), stop=False, skip_group_check=True)
                    ins = tensor.matmul(
                        out=pmm[bank][:], lhsT=ones_row_b[:],
                        rhs=bvr[:, vh * 512:(vh + 1) * 512],
                        start=False, stop=True, skip_group_check=True)
                    c_tn.inc(ins)
                else:
                    dq = idx
                    bank = dq % 2
                    (c_vx if pmm_free[bank][0] == "vx" else c_sx).wait(tensor, pmm_free[bank][1])
                    for kc in range(DC):
                        ins = tensor.matmul(
                            out=pmm[bank][:, 0:256],
                            lhsT=Wq_sb[:, kc * 1024 + dq * 128: kc * 1024 + dq * 128 + 128],
                            rhs=xlnb[sb][:, kc * 512: kc * 512 + 256],
                            start=(kc == 0), stop=(kc == DC - 1), skip_group_check=True)
                    c_tn.inc(ins)
            block.tensor(proj)
            first_proj = False
            tn_here = c_tn.n

            if kind in ("k", "q"):
                def evac1(vector, kind=kind, idx=idx, h=h, tn_here=tn_here):
                    c_tn.wait(vector, tn_here)
                    bank = idx % 2
                    if kind == "k":
                        dk = idx
                        ins = vector.tensor_scalar(
                            out=K_sb[:, dk * 1024 + h * 512: dk * 1024 + h * 512 + 512],
                            in0=pmm[bank][:], scalar1=bkp[:, dk:dk + 1], scalar2=0.0,
                            op0=ALU.add, op1=ALU.max)
                    else:
                        dq = idx
                        ins = vector.tensor_scalar(
                            out=Qc_sb[:, dq * 256:(dq + 1) * 256],
                            in0=pmm[bank][:, 0:256], scalar1=bqp[:, dq:dq + 1], scalar2=0.0,
                            op0=ALU.add, op1=ALU.max)
                    c_vx.inc(ins)
                    pmm_free[bank] = ("vx", c_vx.n)
                block.vector(evac1)
            else:
                def evac1(scalar, idx=idx, h=h, tn_here=tn_here):
                    c_tn.wait(scalar, tn_here)
                    tt, vh = idx
                    bank = 2 + (tt * 2 + vh) % 2
                    tile = 4 * h + tt
                    c_sx.inc(scalar.activation(
                        out=V_sb[:, tile * 1024 + vh * 512: tile * 1024 + vh * 512 + 512],
                        in_=pmm[bank][:], func=AF.Relu))
                    pmm_free[bank] = ("sx", c_sx.n)
                block.scalar(evac1)
        xlnb_free[sb] = c_tn.n
        return (c_sx.n, c_vx.n)

    st_state = [0]
    halves = [(c, h) for c in range(CH) for h in range(2)]
    vx_lns = {}
    vx_lns[0] = a0_prep(*halves[0], 0)
    for i, (c, h) in enumerate(halves):
        if i + 1 < len(halves):
            c1, h1 = halves[i + 1]
            vx_lns[i + 1] = a0_prep(c1, h1, (i + 1) % 2)
        sx_evac = a0_proj(c, h, i % 2, vx_lns[i])
        if h == 1:
            def kv_store(sync, c=c, sx_evac=sx_evac):
                c_sx.wait(sync, sx_evac[0])
                c_vx.wait(sync, sx_evac[1])
                c_st.inc(sync.dma_start(out=k_scr[c], in_=K_sb[:]), 16)
                c_st.inc(sync.dma_start(out=v_scr[c], in_=V_sb[:]), 16)
                dst = q_scr[:, :, 256 * (7 - c): 256 * (8 - c)]
                c_st.inc(sync.dma_start(
                    out=dst, in_=Qc_sb.rearrange("p (dq n) -> p dq n", dq=DC)), 16)
            block.sync(kv_store)
            st_state[0] = c_st.n

    ctxA0.close()
    barrier()

    resid = E(nc.sbuf_tensor("resid", [128, NT * 1024], F32))

    ctxA1 = ExitStack()
    EA1 = ctxA1.enter_context
    Kbuf = [EA1(nc.sbuf_tensor(f"Kbuf{b}", [128, DC * 1024], BF16)) for b in range(2)]
    Vbuf = [EA1(nc.sbuf_tensor(f"Vbuf{b}", [128, 8 * 1024], BF16)) for b in range(2)]
    Qp_sb = EA1(nc.sbuf_tensor("Qp_sb", [128, DC * 256], BF16))
    P_sb = EA1(nc.sbuf_tensor("P_sb", [128, 8 * 256], BF16))
    am_sb = EA1(nc.sbuf_tensor("am_sb", [128, 8 * 256], F32))
    Wo_sb = EA1(nc.sbuf_tensor("Wo_sb", [128, DC * 1024], BF16))
    bor = EA1(nc.sbuf_tensor("bor", [1, D], BF16))
    lsb = EA1(nc.sbuf_tensor("lsb", [128, 2], F32))
    rl = EA1(nc.sbuf_tensor("rl", [128, 2], F32))
    ybf = EA1(nc.sbuf_tensor("ybf", [128, 2 * 1024], BF16))
    yfm = EA1(nc.sbuf_tensor("yfm", [128, DC * 256], BF16))
    orel = EA1(nc.sbuf_tensor("orel", [128, 512], F32))
    xo_sb = EA1(nc.sbuf_tensor("xo_sb", [128, 2 * 1024], F32))

    pl = EA1(nc.psum_tensor("pl", [128, 2], F32))
    pav = [EA1(nc.psum_tensor(f"pav{i}", [128, 512], F32)) for i in range(4)]
    ps = [EA1(nc.psum_tensor(f"psA1{i}", [128, 256], F32)) for i in range(2)]
    ptr = EA1(nc.psum_tensor("ptrA1", [128, 256], BF16))

    def a1_weights(gpsimd):
        c_lw.wait(gpsimd)
        for dc in range(DC):
            c_lw.inc(gpsimd.dma_start(
                out=Wo_sb[:, dc * 1024:(dc + 1) * 1024],
                in_=Wo[dc * 128:(dc + 1) * 128, :]), 16)
        c_lw.inc(gpsimd.dma_start(out=bor[:], in_=bo_row), 16)
    block.gpsimd(a1_weights)
    lw_a1 = c_lw.n

    st_a1_guard = c_st.n
    kbuf_free = [0] * 2
    vbuf_free = [0] * 2
    ps_free = [0] * 2
    am_free = 0
    qp_free = 0
    xo_free = 0
    pl_free = 0
    pav_free = [0] * 4
    ybf_free = 0
    rl_free = 0
    ptr_free = 0
    orel_free = 0
    first_sync_a1 = True

    for pair in range(CH):
        n_ch = 8 - pair

        def q_load(sync, pair=pair):
            nonlocal first_sync_a1
            if first_sync_a1:
                c_st.wait(sync, st_a1_guard)
                first_sync_a1 = False
            c_tn.wait(sync, qp_free)
            src = q_scr[:, :, 256 * pair:256 * (pair + 1)]
            s_qp.inc(sync.dma_start(
                out=Qp_sb.rearrange("p (dq n) -> p dq n", dq=DC), in_=src), 16)
            c_vx.wait(sync, xo_free)
            s_qp.inc(sync.dma_start(
                out=xo_sb.rearrange("p (t d) -> p t d", t=2),
                in_=xown.rearrange("(t p) d -> p t d", p=128)[:, 2 * pair:2 * pair + 2]), 16)
        block.sync(q_load)
        ld_q = s_qp.n

        def m_load(sync, pair=pair):
            c_vx.wait(sync, am_free)
            src = amask[7 - pair].rearrange("(kt p) n -> p kt n", p=128)
            s_am.inc(sync.dma_start(
                out=am_sb.rearrange("p (kt n) -> p kt n", kt=8), in_=src), 16)
        block.sync(m_load)
        ld_m = s_am.n

        for cv in range(n_ch):
            sbuf = cv % 2
            diag = (cv == n_ch - 1)

            def kv_load(sync, cv=cv, sbuf=sbuf):
                c_tn.wait(sync, kbuf_free[sbuf])
                s_kv[sbuf].inc(sync.dma_start(out=Kbuf[sbuf][:], in_=k_scr[cv]), 16)
                c_tn.wait(sync, vbuf_free[sbuf])
                s_kv[sbuf].inc(sync.dma_start(out=Vbuf[sbuf][:], in_=v_scr[cv]), 16)
            block.sync(kv_load)
            ld_kv = s_kv[sbuf].n

            p_ready = []
            for kt in range(8):
                def score(tensor, kt=kt, sbuf=sbuf, ld_kv=ld_kv, ld_q=ld_q):
                    if kt == 0:
                        s_kv[sbuf].wait(tensor, ld_kv)
                        s_qp.wait(tensor, ld_q)
                    bank = kt % 2
                    c_sx.wait(tensor, ps_free[bank])
                    for dc in range(DC):
                        ins = tensor.matmul(
                            out=ps[bank][:],
                            lhsT=Kbuf[sbuf][:, dc * 1024 + kt * 128: dc * 1024 + kt * 128 + 128],
                            rhs=Qp_sb[:, dc * 256:(dc + 1) * 256],
                            start=(dc == 0), stop=(dc == DC - 1), skip_group_check=True)
                    c_tn.inc(ins)
                block.tensor(score)
                tn_sc = c_tn.n

                vx_m = 0
                if diag:
                    def maskadd(vector, kt=kt, tn_sc=tn_sc, ld_m=ld_m):
                        c_tn.wait(vector, tn_sc)
                        s_am.wait(vector, ld_m)
                        bank = kt % 2
                        c_vx.inc(vector.tensor_tensor(
                            out=ps[bank][:], in0=ps[bank][:],
                            in1=am_sb[:, kt * 256:(kt + 1) * 256], op=ALU.add))
                    block.vector(maskadd)
                    vx_m = c_vx.n

                def expev(scalar, kt=kt, diag=diag, tn_sc=tn_sc, vx_m=vx_m):
                    bank = kt % 2
                    if diag:
                        c_vx.wait(scalar, vx_m)
                    else:
                        c_tn.wait(scalar, tn_sc)
                    c_sx.inc(scalar.activation(
                        out=P_sb[:, kt * 256:(kt + 1) * 256], in_=ps[bank][:],
                        func=AF.Exp, bias=negC[:], scale=inv32[:]))
                    ps_free[bank] = c_sx.n
                block.scalar(expev)
                p_ready.append(c_sx.n)
            if diag:
                am_free = c_vx.n

            def av(tensor, cv=cv, sbuf=sbuf, n_ch=n_ch, p_ready=p_ready):
                if cv == 0:
                    c_vx.wait(tensor, pl_free)
                    c_sx.wait(tensor, max(pav_free))
                ins = None
                for qt in range(2):
                    for kt in range(8):
                        c_sx.wait(tensor, p_ready[kt])
                        tensor.matmul(
                            out=pl[:, qt:qt + 1],
                            lhsT=P_sb[:, kt * 256 + qt * 128: kt * 256 + qt * 128 + 128],
                            rhs=ones_col[:],
                            start=(cv == 0 and kt == 0 and qt == 0),
                            stop=(cv == n_ch - 1 and kt == 7),
                            skip_group_check=True)
                        for half in range(2):
                            ins = tensor.matmul(
                                out=pav[qt * 2 + half][:],
                                lhsT=P_sb[:, kt * 256 + qt * 128: kt * 256 + qt * 128 + 128],
                                rhs=Vbuf[sbuf][:, kt * 1024 + half * 512: kt * 1024 + (half + 1) * 512],
                                start=(cv == 0 and kt == 0),
                                stop=(cv == n_ch - 1 and kt == 7),
                                skip_group_check=True)
                c_tn.inc(ins)
                kbuf_free[sbuf] = c_tn.n
                vbuf_free[sbuf] = c_tn.n
            block.tensor(av)
        tn_av = c_tn.n
        qp_free_next = c_tn.n

        def fin_v(vector):
            c_tn.wait(vector, tn_av)
            c_sx.wait(vector, rl_free)
            c_vx.inc(vector.tensor_copy(out=lsb[:], in_=pl[:]))
            c_vx.wait(vector)
            c_vx.inc(vector.reciprocal(out=rl[:], in_=lsb[:]))
        block.vector(fin_v)
        pl_free = c_vx.n
        vx_rl = c_vx.n

        def fin_s(scalar):
            c_vx.wait(scalar, vx_rl)
            c_tn.wait(scalar, ybf_free)
            for qt in range(2):
                for half in range(2):
                    c_sx.inc(scalar.activation(
                        out=ybf[:, qt * 1024 + half * 512: qt * 1024 + (half + 1) * 512],
                        in_=pav[qt * 2 + half][:], func=AF.Copy, scale=rl[:, qt:qt + 1]))
        block.scalar(fin_s)
        sx_ybf = c_sx.n
        rl_free = c_sx.n
        for i in range(4):
            pav_free[i] = sx_ybf

        tr_done = []
        tr_evac = []
        for dc in range(DC):
            def tr(tensor, dc=dc):
                if dc == 0:
                    c_sx.wait(tensor, sx_ybf)
                    c_ld.wait(tensor)
                    c_vx.wait(tensor, ptr_free)
                else:
                    c_vx.wait(tensor, tr_evac[dc - 1])
                ins = None
                for qt in range(2):
                    ins = tensor.transpose(
                        out=ptr[:, qt * 128:(qt + 1) * 128],
                        in_=ybf[:, qt * 1024 + dc * 128: qt * 1024 + dc * 128 + 128],
                        identity=ident[:])
                c_tn.inc(ins)
                tr_done.append(c_tn.n)

            def trev(vector, dc=dc):
                c_tn.wait(vector, tr_done[dc])
                c_vx.inc(vector.tensor_copy(out=yfm[:, dc * 256:(dc + 1) * 256],
                                            in_=ptr[:]))
                tr_evac.append(c_vx.n)
            block.tensor(tr)
            block.vector(trev)
        vx_yfm = c_vx.n
        ptr_free = c_vx.n
        ybf_free = tr_done[-1]

        tn_op = []

        def oproj(tensor, pair=pair):
            c_vx.wait(tensor, vx_yfm)
            if pair == 0:
                c_lw.wait(tensor, lw_a1)
            for tt in range(2):
                for half in range(2):
                    bank = tt * 2 + half
                    c_sx.wait(tensor, pav_free[bank])
                    for dc in range(DC):
                        tensor.matmul(
                            out=pav[bank][:],
                            lhsT=yfm[:, dc * 256 + tt * 128: dc * 256 + tt * 128 + 128],
                            rhs=Wo_sb[:, dc * 1024 + half * 512: dc * 1024 + (half + 1) * 512],
                            start=(dc == 0), stop=False, skip_group_check=True)
                    ins = tensor.matmul(
                        out=pav[bank][:], lhsT=ones_row_b[:],
                        rhs=bor[:, half * 512:(half + 1) * 512],
                        start=False, stop=True, skip_group_check=True)
                    c_tn.inc(ins)
                    tn_op.append((c_tn.n, bank, tt, half))
        block.tensor(oproj)

        sx_orel = []
        res_done = []
        for i in range(4):
            def s_one(scalar, i=i):
                tnv, bank, tt, half = tn_op[i]
                c_tn.wait(scalar, tnv)
                if i >= 1:
                    c_vx.wait(scalar, res_done[i - 1])
                else:
                    c_vx.wait(scalar, orel_free)
                c_sx.inc(scalar.activation(out=orel[:], in_=pav[bank][:], func=AF.Relu))
                sx_orel.append((c_sx.n, tt, half))
                pav_free[bank] = c_sx.n

            def v_one(vector, i=i, pair=pair, ld_q=ld_q):
                sxv, tt, half = sx_orel[i]
                if i == 0:
                    s_qp.wait(vector, ld_q)
                c_sx.wait(vector, sxv)
                g = 2 * pair + tt
                c_vx.inc(vector.tensor_tensor(
                    out=resid[:, g * 1024 + half * 512: g * 1024 + (half + 1) * 512],
                    in0=xo_sb[:, tt * 1024 + half * 512: tt * 1024 + (half + 1) * 512],
                    in1=orel[:], op=ALU.add))
                res_done.append(c_vx.n)
            block.scalar(s_one)
            block.vector(v_one)
        orel_free = c_vx.n
        xo_free = c_vx.n
        qp_free = qp_free_next

    ctxA1.close()
    barrier()

    s_w01 = [Ctr(ctx, nc, f"s_w01{i}") for i in range(2)]
    s_w2 = [Ctr(ctx, nc, f"s_w2{i}") for i in range(2)]
    dma_ctrs += s_w01 + s_w2
    ffn_state = {"w01_round": 0, "w2_round": 0}


    def emit_ln_fm(tag, x_ln2, sq, xlt, s1c, s2c, meanc, tc, msqc, varc, stdc,
                   rstdc, ptrB):
        xlt_free = [0, 0]
        ptr_free_B = [0]
        for tt in range(NT):
            def ln_s(scalar, tt=tt):
                c_sx.inc(scalar.activation(out=sq[:], in_=resid[:, tt * 1024:(tt + 1) * 1024],
                                           func=AF.Square))
            def ln_v1(vector, tt=tt):
                c_vx.inc(vector.reduce_sum(out=s1c[:], in_=resid[:, tt * 1024:(tt + 1) * 1024],
                                           axis=AX.X))
            block.scalar(ln_s)
            sx_sq2 = c_sx.n
            block.vector(ln_v1)
            def ln_v2(vector, sx_sq2=sx_sq2):
                c_sx.wait(vector, sx_sq2)
                c_vx.inc(vector.reduce_sum(out=s2c[:], in_=sq[:], axis=AX.X))
            block.vector(ln_v2)
            vx_s12 = c_vx.n
            def ln_s2(scalar, vx_s12=vx_s12):
                c_vx.wait(scalar, vx_s12)
                c_sx.inc(scalar.mul(out=meanc[:], in_=s1c[:], mul=1.0 / D))
                scalar.mul(out=tc[:], in_=s2c[:], mul=1.0 / D)
                c_sx.wait(scalar)
                c_sx.inc(scalar.activation(out=msqc[:], in_=meanc[:], func=AF.Square))
            block.scalar(ln_s2)
            sx_ms = c_sx.n
            def ln_v3(vector, sx_ms=sx_ms):
                c_sx.wait(vector, sx_ms)
                c_vx.inc(vector.tensor_tensor(out=varc[:], in0=tc[:], in1=msqc[:],
                                              op=ALU.subtract))
            block.vector(ln_v3)
            vx_var2 = c_vx.n
            def ln_s3(scalar, vx_var2=vx_var2):
                c_vx.wait(scalar, vx_var2)
                c_sx.inc(scalar.activation(out=stdc[:], in_=varc[:], func=AF.Sqrt,
                                           bias=epsc[:]))
            block.scalar(ln_s3)
            sx_std2 = c_sx.n
            def ln_v4(vector, tt=tt, sx_std2=sx_std2):
                c_sx.wait(vector, sx_std2)
                c_vx.inc(vector.reciprocal(out=rstdc[:], in_=stdc[:]))
                c_vx.wait(vector)
                c_tn.wait(vector, xlt_free[tt % 2])
                c_vx.inc(vector.tensor_scalar(
                    out=xlt[tt % 2][:], in0=resid[:, tt * 1024:(tt + 1) * 1024],
                    scalar1=meanc[:], scalar2=rstdc[:],
                    op0=ALU.subtract, op1=ALU.mult))
            block.vector(ln_v4)
            vx_xlt = c_vx.n
            for b4 in range(2):
                def tr_b(tensor, tt=tt, b4=b4, vx_xlt=vx_xlt):
                    c_vx.wait(tensor, vx_xlt)
                    c_vx.wait(tensor, ptr_free_B[0])
                    c_ld.wait(tensor)
                    ins = None
                    for j in range(4):
                        dc = b4 * 4 + j
                        ins = tensor.transpose(
                            out=ptrB[:, j * 128:(j + 1) * 128],
                            in_=xlt[tt % 2][:, dc * 128:(dc + 1) * 128],
                            identity=ident[:])
                    c_tn.inc(ins)
                block.tensor(tr_b)
                tn_tr = c_tn.n
                def trev_b(vector, tt=tt, b4=b4, tn_tr=tn_tr):
                    c_tn.wait(vector, tn_tr)
                    dst = x_ln2.rearrange("p (dc n) -> p dc n", dc=DC)
                    c_vx.inc(vector.tensor_copy(
                        out=dst[:, b4 * 4:(b4 + 1) * 4, tt * 128:(tt + 1) * 128],
                        in_=ptrB.rearrange("p (j n) -> p j n", j=4)))
                    ptr_free_B[0] = c_vx.n
                block.vector(trev_b)
            xlt_free[tt % 2] = c_tn.n
        return c_vx.n, c_sx.n

    def emit_ffn(tag):
        ctxF = ExitStack()
        EF = ctxF.enter_context
        x_ln2 = EF(nc.sbuf_tensor(f"xln2{tag}", [128, DC * 2048], BF16))
        c_all = EF(nc.sbuf_tensor(f"call{tag}", [128, NT * 8], F32))

        ctxG = ExitStack()
        EG = ctxG.enter_context
        sq = EG(nc.sbuf_tensor(f"sq{tag}", [128, 1024], F32))
        xlt = [EG(nc.sbuf_tensor(f"xlt{tag}{b}", [128, 1024], BF16)) for b in range(2)]
        s1c = EG(nc.sbuf_tensor(f"s1c{tag}", [128, 1], F32))
        s2c = EG(nc.sbuf_tensor(f"s2c{tag}", [128, 1], F32))
        meanc = EG(nc.sbuf_tensor(f"meanc{tag}", [128, 1], F32))
        tc = EG(nc.sbuf_tensor(f"tc{tag}", [128, 1], F32))
        msqc = EG(nc.sbuf_tensor(f"msqc{tag}", [128, 1], F32))
        varc = EG(nc.sbuf_tensor(f"varc{tag}", [128, 1], F32))
        stdc = EG(nc.sbuf_tensor(f"stdc{tag}", [128, 1], F32))
        rstdc = EG(nc.sbuf_tensor(f"rstdc{tag}", [128, 1], F32))
        Wg_sb = EG(nc.sbuf_tensor(f"Wg{tag}", [128, DC * 8], BF16))
        bgb = EG(nc.sbuf_tensor(f"bgb{tag}", [128, 8], F32))
        glog = EG(nc.sbuf_tensor(f"glog{tag}", [128, 8], F32))
        m1 = EG(nc.sbuf_tensor(f"m1{tag}", [128, 1], F32))
        negm1 = EG(nc.sbuf_tensor(f"negm1{tag}", [128, 1], F32))
        m2 = EG(nc.sbuf_tensor(f"m2{tag}", [128, 1], F32))
        em = EG(nc.sbuf_tensor(f"em{tag}", [128, 1], F32))
        Zc = EG(nc.sbuf_tensor(f"Zc{tag}", [128, 1], F32))
        rZ = EG(nc.sbuf_tensor(f"rZ{tag}", [128, 1], F32))
        eq = EG(nc.sbuf_tensor(f"eq{tag}", [128, 8], F32))
        gm = EG(nc.sbuf_tensor(f"gm{tag}", [128, 8], F32))
        ge = EG(nc.sbuf_tensor(f"ge{tag}", [128, 8], F32))
        e1 = EG(nc.sbuf_tensor(f"e1{tag}", [128, 8], F32))
        cbf = EG(nc.sbuf_tensor(f"cbf{tag}", [128, 8], BF16))
        cT_sb = EG(nc.sbuf_tensor(f"cT{tag}", [8, NT * 128], BF16))
        b2_sb = EG(nc.sbuf_tensor(f"b2{tag}", [8, D], BF16))
        pgate = EG(nc.psum_tensor(f"pgate{tag}", [128, 8], F32))
        pct = EG(nc.psum_tensor(f"pct{tag}", [8, 128], BF16))
        ptrB = EG(nc.psum_tensor(f"ptrB{tag}", [128, 512], BF16))
        pb = [EG(nc.psum_tensor(f"pb{tag}{i}", [128, 512], F32)) for i in range(2)]

        def g_loads(gpsimd):
            c_lw.wait(gpsimd)
            for kc in range(DC):
                c_lw.inc(gpsimd.dma_start(
                    out=Wg_sb[:, kc * 8:(kc + 1) * 8],
                    in_=Wg[kc * 128:(kc + 1) * 128, :]), 16)
            c_lw.inc(gpsimd.dma_start(out=b2_sb[:], in_=b2_in), 16)
        block.gpsimd(g_loads)
        lw_g = c_lw.n

        def g_loads2(sync):
            c_ld.wait(sync)
            c_ld.inc(sync.dma_start(out=bgb[:], in_=bg_bc), 16)
        block.sync(g_loads2)
        ld_g = c_ld.n

        vx_ln_all, sx_ln_all = emit_ln_fm(tag, x_ln2, sq, xlt,
            s1c, s2c, meanc, tc, msqc, varc, stdc, rstdc, ptrB)

        for tt in range(NT):
            def gt_mm(tensor, tt=tt):
                if tt == 0:
                    c_lw.wait(tensor, lw_g)
                    c_vx.wait(tensor, vx_ln_all)
                for kc in range(DC):
                    ins = tensor.matmul(
                        out=pgate[:], lhsT=x_ln2[:, kc * 2048 + tt * 128: kc * 2048 + tt * 128 + 128],
                        rhs=Wg_sb[:, kc * 8:(kc + 1) * 8],
                        start=(kc == 0), stop=(kc == DC - 1), skip_group_check=True)
                c_tn.inc(ins)
            block.tensor(gt_mm)
            tn_g = c_tn.n

            def gt_v(vector, tt=tt, tn_g=tn_g):
                c_tn.wait(vector, tn_g)
                if tt == 0:
                    c_ld.wait(vector)
                c_vx.inc(vector.tensor_tensor(out=glog[:], in0=pgate[:], in1=bgb[:],
                                              op=ALU.add))
                c_vx.wait(vector)
                c_vx.inc(vector.reduce_max(out=m1[:], in_=glog[:], axis=AX.X))
                c_vx.inc(vector.reduce_max(out=negm1[:], in_=glog[:], axis=AX.X, negate=True))
                c_vx.wait(vector)
                c_vx.inc(vector.tensor_scalar(out=eq[:], in0=glog[:], scalar1=m1[:],
                                              scalar2=None, op0=ALU.is_equal))
                c_vx.wait(vector)
                c_vx.inc(vector.scalar_tensor_tensor(out=gm[:], in0=eq[:], scalar=-1e30,
                                                     in1=glog[:], op0=ALU.mult, op1=ALU.add))
                c_vx.wait(vector)
                c_vx.inc(vector.reduce_max(out=m2[:], in_=gm[:], axis=AX.X))
            block.vector(gt_v)
            vx_g1 = c_vx.n

            def gt_s(scalar, tt=tt, vx_g1=vx_g1):
                c_vx.wait(scalar, vx_g1)
                c_sx.inc(scalar.activation(out=em[:], in_=m2[:], func=AF.Exp, bias=negm1[:]))
                c_sx.wait(scalar)
                c_sx.inc(scalar.add(out=Zc[:], in_=em[:], add=1.0))
                c_sx.inc(scalar.activation(out=e1[:], in_=glog[:], func=AF.Exp, bias=negm1[:]))
            block.scalar(gt_s)
            sx_g1 = c_sx.n

            def gt_v2(vector, tt=tt, sx_g1=sx_g1):
                c_sx.wait(vector, sx_g1)
                c_vx.inc(vector.reciprocal(out=rZ[:], in_=Zc[:]))
                c_vx.inc(vector.tensor_scalar(out=ge[:], in0=glog[:], scalar1=m2[:],
                                              scalar2=None, op0=ALU.is_ge))
                c_vx.wait(vector)
                c_vx.inc(vector.scalar_tensor_tensor(
                    out=c_all[:, tt * 8:(tt + 1) * 8], in0=e1[:], scalar=rZ[:],
                    in1=ge[:], op0=ALU.mult, op1=ALU.mult))
                c_vx.wait(vector)
                c_vx.inc(vector.tensor_copy(out=cbf[:], in_=c_all[:, tt * 8:(tt + 1) * 8]))
            block.vector(gt_v2)
            vx_cbf = c_vx.n

            def gt_tr(tensor, tt=tt, vx_cbf=vx_cbf):
                c_vx.wait(tensor, vx_cbf)
                c_tn.inc(tensor.transpose(out=pct[:], in_=cbf[:], identity=ident[:]))
            block.tensor(gt_tr)
            tn_ct = c_tn.n

            def gt_v3(vector, tt=tt, tn_ct=tn_ct):
                c_tn.wait(vector, tn_ct)
                c_vx.inc(vector.tensor_copy(out=cT_sb[:, tt * 128:(tt + 1) * 128], in_=pct[:]))
            block.vector(gt_v3)
        vx_gate_all = c_vx.n

        tn_b2 = []
        pb_free = [0, 0]
        for tt in range(NT):
            def b2_mm(tensor, tt=tt):
                if tt == 0:
                    c_vx.wait(tensor, vx_gate_all)
                    c_sx.wait(tensor, sx_ln_all)
                for half in range(2):
                    c_vx.wait(tensor, pb_free[half])
                    c_tn.inc(tensor.matmul(
                        out=pb[half][:], lhsT=cT_sb[:, tt * 128:(tt + 1) * 128],
                        rhs=b2_sb[:, half * 512:(half + 1) * 512],
                        start=True, stop=True, skip_group_check=True))
                    tn_b2.append(c_tn.n)
            block.tensor(b2_mm)

            def b2_v(vector, tt=tt):
                for half in range(2):
                    c_tn.wait(vector, tn_b2[tt * 2 + half])
                    sl = resid[:, tt * 1024 + half * 512: tt * 1024 + (half + 1) * 512]
                    c_vx.inc(vector.tensor_tensor(out=sl, in0=sl, in1=pb[half][:],
                                                  op=ALU.add))
                    pb_free[half] = c_vx.n
            block.vector(b2_v)

        vx_g_end = c_vx.n
        sx_g_end = c_sx.n
        tn_g_end = c_tn.n
        ctxG.close()

        ctxE = ExitStack()
        EE = ctxE.enter_context
        hg = EE(nc.sbuf_tensor(f"hg{tag}", [128, 16 * 2048], BF16))
        w0s = [EE(nc.sbuf_tensor(f"w0s{tag}{b}", [128, DC * 128], BF16)) for b in range(2)]
        w1s = [EE(nc.sbuf_tensor(f"w1s{tag}{b}", [128, DC * 128], BF16)) for b in range(2)]
        w2s = [EE(nc.sbuf_tensor(f"w2s{tag}{b}", [128, 16 * 512], BF16)) for b in range(2)]
        gsil = [EE(nc.sbuf_tensor(f"gsil{tag}{b}", [128, 512], BF16)) for b in range(2)]
        b0p = EE(nc.sbuf_tensor(f"b0p{tag}", [128, 256], F32))
        b1p = EE(nc.sbuf_tensor(f"b1p{tag}", [128, 256], F32))
        ph = [EE(nc.psum_tensor(f"ph{tag}{i}", [128, 512], F32)) for i in range(2)]
        pg2 = [EE(nc.psum_tensor(f"pg2{tag}{i}", [128, 512], F32)) for i in range(2)]
        py = [EE(nc.psum_tensor(f"py{tag}{i}", [128, 512], F32)) for i in range(2)]

        def e_loads(sync):
            c_ld.wait(sync)
            c_vx.wait(sync, vx_g_end)
            c_sx.wait(sync, sx_g_end)
            c_tn.wait(sync, tn_g_end)
            c_ld.inc(sync.dma_start(out=b0p[:], in_=b0_pt), 16)
            c_ld.inc(sync.dma_start(out=b1p[:], in_=b1_pt), 16)
        block.sync(e_loads)
        ld_e = c_ld.n

        w01_free = [tn_g_end, tn_g_end]
        w2_free = [tn_g_end, tn_g_end]
        ph_free = [vx_g_end, vx_g_end]
        pg_free = [sx_g_end, sx_g_end]
        gs_free = [vx_g_end, vx_g_end]
        py_free = [vx_g_end, vx_g_end]
        hg_free = tn_g_end
        w2i = 0

        for e in range(NEXP):
            for hh in range(2):
                vx_hg0 = c_vx.n
                for ht16 in range(16):
                    ht = hh * 16 + ht16
                    wb = ht16 % 2

                    def w_load(gpsimd, e=e, ht=ht, wb=wb):
                        c_tn.wait(gpsimd, w01_free[wb])
                        src0 = W0[e][:, ht * 128:(ht + 1) * 128]
                        src0 = src0.rearrange("(kc p) n -> p kc n", p=128)
                        s_w01[wb].inc(gpsimd.dma_start(
                            out=w0s[wb].rearrange("p (kc n) -> p kc n", kc=DC), in_=src0), 16)
                        src1 = W1[e][:, ht * 128:(ht + 1) * 128]
                        src1 = src1.rearrange("(kc p) n -> p kc n", p=128)
                        s_w01[wb].inc(gpsimd.dma_start(
                            out=w1s[wb].rearrange("p (kc n) -> p kc n", kc=DC), in_=src1), 16)
                    block.gpsimd(w_load)
                    ld_w = s_w01[wb].n

                    for tq in range(4):
                        qb = tq % 2

                        def hgmm(tensor, e=e, ht16=ht16, tq=tq, wb=wb, qb=qb, ld_w=ld_w):
                            if tq == 0:
                                s_w01[wb].wait(tensor, ld_w)
                                if ht16 == 0:
                                    c_tn.wait(tensor, 0)
                            c_vx.wait(tensor, ph_free[qb])
                            for kc in range(DC):
                                ins = tensor.matmul(
                                    out=ph[qb][:],
                                    lhsT=w0s[wb][:, kc * 128:(kc + 1) * 128],
                                    rhs=x_ln2[:, kc * 2048 + tq * 512: kc * 2048 + (tq + 1) * 512],
                                    start=(kc == 0), stop=(kc == DC - 1), skip_group_check=True)
                            c_tn.inc(ins)
                            c_sx.wait(tensor, pg_free[qb])
                            for kc in range(DC):
                                ins = tensor.matmul(
                                    out=pg2[qb][:],
                                    lhsT=w1s[wb][:, kc * 128:(kc + 1) * 128],
                                    rhs=x_ln2[:, kc * 2048 + tq * 512: kc * 2048 + (tq + 1) * 512],
                                    start=(kc == 0), stop=(kc == DC - 1), skip_group_check=True)
                            c_tn.inc(ins)
                        block.tensor(hgmm)
                        tn_hg = c_tn.n
                        if tq == 3:
                            w01_free[wb] = c_tn.n

                        def sil(scalar, e=e, ht=ht, qb=qb, tn_hg=tn_hg):
                            c_tn.wait(scalar, tn_hg)
                            if e == 0 and ht == 0:
                                c_ld.wait(scalar, ld_e)
                            c_vx.wait(scalar, gs_free[qb])
                            c_sx.inc(scalar.activation(
                                out=gsil[qb][:], in_=pg2[qb][:], func=AF.Silu,
                                bias=b1p[:, e * 32 + ht: e * 32 + ht + 1]))
                            pg_free[qb] = c_sx.n
                        block.scalar(sil)
                        sx_sil = c_sx.n

                        def hgv(vector, e=e, ht=ht, ht16=ht16, tq=tq, qb=qb, sx_sil=sx_sil):
                            c_sx.wait(vector, sx_sil)
                            if e == 0 and ht == 0 and tq == 0:
                                c_ld.wait(vector, ld_e)
                            if ht16 == 0 and tq == 0:
                                c_tn.wait(vector, hg_free)
                            c_vx.inc(vector.scalar_tensor_tensor(
                                out=hg[:, ht16 * 2048 + tq * 512: ht16 * 2048 + (tq + 1) * 512],
                                in0=ph[qb][:], scalar=b0p[:, e * 32 + ht: e * 32 + ht + 1],
                                in1=gsil[qb][:], op0=ALU.add, op1=ALU.mult))
                            ph_free[qb] = c_vx.n
                            gs_free[qb] = c_vx.n
                        block.vector(hgv)
                vx_hg_all = c_vx.n

                for dmh in range(2):
                    w2b = w2i % 2

                    def w2_load(gpsimd, e=e, hh=hh, dmh=dmh, w2b=w2b):
                        c_tn.wait(gpsimd, w2_free[w2b])
                        src = W2[e][hh * 2048:(hh + 1) * 2048, dmh * 512:(dmh + 1) * 512]
                        src = src.rearrange("(ht p) n -> p ht n", p=128)
                        s_w2[w2b].inc(gpsimd.dma_start(
                            out=w2s[w2b].rearrange("p (ht n) -> p ht n", ht=16), in_=src), 16)
                    block.gpsimd(w2_load)
                    ld_w2 = s_w2[w2b].n
                    w2i += 1

                    for tt in range(NT):
                        yb = tt % 2

                        def ymm(tensor, tt=tt, w2b=w2b, yb=yb, ld_w2=ld_w2,
                                vx_hg_all=vx_hg_all):
                            if tt == 0:
                                s_w2[w2b].wait(tensor, ld_w2)
                                c_vx.wait(tensor, vx_hg_all)
                            c_vx.wait(tensor, py_free[yb])
                            for ht16 in range(16):
                                ins = tensor.matmul(
                                    out=py[yb][:],
                                    lhsT=hg[:, ht16 * 2048 + tt * 128: ht16 * 2048 + tt * 128 + 128],
                                    rhs=w2s[w2b][:, ht16 * 512:(ht16 + 1) * 512],
                                    start=(ht16 == 0), stop=(ht16 == 15), skip_group_check=True)
                            c_tn.inc(ins)
                        block.tensor(ymm)
                        tn_y = c_tn.n
                        if tt == NT - 1:
                            w2_free[w2b] = c_tn.n
                            if dmh == 1:
                                hg_free = c_tn.n

                        def yv(vector, e=e, tt=tt, dmh=dmh, yb=yb, tn_y=tn_y):
                            c_tn.wait(vector, tn_y)
                            sl = resid[:, tt * 1024 + dmh * 512: tt * 1024 + (dmh + 1) * 512]
                            c_vx.inc(vector.scalar_tensor_tensor(
                                out=sl, in0=py[yb][:],
                                scalar=c_all[:, tt * 8 + e: tt * 8 + e + 1],
                                in1=sl, op0=ALU.mult, op1=ALU.add))
                            py_free[yb] = c_vx.n
                        block.vector(yv)

        ctxE.close()
        ctxF.close()


    def emit_attn_b(level=3):
        cc_in_k = nc.dram_tensor("cc_in_k", [1024, 128], BF16)
        cc_out_k = nc.dram_tensor("cc_out_k", [4096, 128], BF16)
        cc_in_v = nc.dram_tensor("cc_in_v", [128, 1040], BF16)
        cc_out_v = nc.dram_tensor("cc_out_v", [512, 1040], BF16)

        ctxC = ExitStack()
        EC = ctxC.enter_context
        q_sb = EC(nc.sbuf_tensor("q_sb", [128, DC * 2048], BF16))
        k_sb = EC(nc.sbuf_tensor("k_sb", [128, DC * 2048], BF16))
        vaug = EC(nc.sbuf_tensor("vaug", [128, NT * 1040], BF16))
        inv8 = EC(nc.sbuf_tensor("inv8", [128, 1], F32))

        def initc(vector):
            c_vx.inc(vector.memset(inv8[:], 0.125))
            for tile in range(NT):
                va = vaug[:, tile * 1040:(tile + 1) * 1040]
                va = va.rearrange("p (h w) -> p h w", h=16)
                c_vx.inc(vector.memset(va[:, :, 64:65], 1.0))
        block.vector(initc)
        vx_init = [c_vx.n]

        ctxQ = ExitStack()
        EQ = ctxQ.enter_context
        x_ln3 = EQ(nc.sbuf_tensor("xln3", [128, DC * 2048], BF16))
        ctxLN = ExitStack()
        EL = ctxLN.enter_context
        sq3 = EL(nc.sbuf_tensor("sq3", [128, 1024], F32))
        xlt3 = [EL(nc.sbuf_tensor(f"xlt3{b}", [128, 1024], BF16)) for b in range(2)]
        cols3 = [EL(nc.sbuf_tensor(f"c3{i}", [128, 1], F32)) for i in range(8)]
        ptrC = EL(nc.psum_tensor("ptrC", [128, 512], BF16))
        vx_ln3, sx_ln3 = emit_ln_fm("C", x_ln3, sq3, xlt3, *cols3, ptrC)
        ctxLN.close()
        wp_free[0] = c_tn.n

        wpart = EQ(nc.sbuf_tensor("wpart", [128, DC * 512], BF16))
        bqkp = EQ(nc.sbuf_tensor("bqkp", [128, 16], F32))
        bv2r = EQ(nc.sbuf_tensor("bv2r", [1, D], BF16))
        pqk = [EQ(nc.psum_tensor(f"pqk{i}", [128, 512], F32)) for i in range(2)]

        def c_bias(sync):
            c_ld.wait(sync)
            c_vx.wait(sync, vx_ln3)
            c_sx.wait(sync, sx_ln3)
            c_tn.wait(sync, wp_free[0])
            c_ld.inc(sync.dma_start(out=bqkp[:], in_=battn_pt), 16)
        block.sync(c_bias)
        ld_cb = c_ld.n

        s_wp = Ctr(ctx, nc, "s_wp")
        dma_ctrs.append(s_wp)
        pqk_free = [0, 0]
        for part in range(2):
            dst = q_sb if part == 0 else k_sb
            for dsub in range(2):
                def wp_load(gpsimd, part=part, dsub=dsub):
                    c_tn.wait(gpsimd, wp_free[0])
                    if part == 0 and dsub == 0:
                        c_vx.wait(gpsimd, vx_ln3)
                        c_sx.wait(gpsimd, sx_ln3)
                    off = part * 1024 + dsub * 512
                    for kc in range(DC):
                        s_wp.inc(gpsimd.dma_start(
                            out=wpart[:, kc * 512:(kc + 1) * 512],
                            in_=Wattn[kc * 128:(kc + 1) * 128, off:off + 512]), 16)
                block.gpsimd(wp_load)
                ld_wp = s_wp.n

                for dt4 in range(4):
                    dt = dsub * 4 + dt4
                    for tg in range(4):
                        def qk_mm(tensor, part=part, dt4=dt4, dt=dt, tg=tg, ld_wp=ld_wp):
                            if dt4 == 0 and tg == 0:
                                s_wp.wait(tensor, ld_wp)
                                c_vx.wait(tensor, vx_ln3)
                            bank = (dt4 * 4 + tg) % 2
                            c_sx.wait(tensor, pqk_free[bank])
                            for kc in range(DC):
                                ins = tensor.matmul(
                                    out=pqk[bank][:],
                                    lhsT=wpart[:, kc * 512 + dt4 * 128: kc * 512 + dt4 * 128 + 128],
                                    rhs=x_ln3[:, kc * 2048 + tg * 512: kc * 2048 + (tg + 1) * 512],
                                    start=(kc == 0), stop=(kc == DC - 1), skip_group_check=True)
                            c_tn.inc(ins)
                        block.tensor(qk_mm)
                        tn_qk = c_tn.n

                        def qk_ev(scalar, part=part, dt4=dt4, dt=dt, tg=tg, tn_qk=tn_qk, dst=dst):
                            c_tn.wait(scalar, tn_qk)
                            if part == 0 and dt4 == 0 and tg == 0 and dt == 0:
                                c_ld.wait(scalar, ld_cb)
                            bank = (dt4 * 4 + tg) % 2
                            c_sx.inc(scalar.activation(
                                out=dst[:, dt * 2048 + tg * 512: dt * 2048 + (tg + 1) * 512],
                                in_=pqk[bank][:], func=AF.Identity,
                                bias=bqkp[:, part * 8 + dt: part * 8 + dt + 1]))
                            pqk_free[bank] = c_sx.n
                        block.scalar(qk_ev)
                wp_free[0] = c_tn.n

        vx_init_done = vx_init[0]
        for vh in range(2):
            def wv_load(gpsimd, vh=vh):
                c_tn.wait(gpsimd, wp_free[0])
                for kc in range(DC):
                    s_wp.inc(gpsimd.dma_start(
                        out=wpart[:, kc * 512:(kc + 1) * 512],
                        in_=Wattn[kc * 128:(kc + 1) * 128, 2048 + vh * 512: 2048 + (vh + 1) * 512]), 16)
                if vh == 0:
                    s_wp.inc(gpsimd.dma_start(out=bv2r[:], in_=battnv_row), 16)
            block.gpsimd(wv_load)
            ld_wv = s_wp.n

            for tile in range(NT):
                def v_mm(tensor, tile=tile, vh=vh, ld_wv=ld_wv):
                    if tile == 0:
                        s_wp.wait(tensor, ld_wv)
                    bank = tile % 2
                    c_sx.wait(tensor, pqk_free[bank])
                    for kc in range(DC):
                        tensor.matmul(
                            out=pqk[bank][:],
                            lhsT=x_ln3[:, kc * 2048 + tile * 128: kc * 2048 + tile * 128 + 128],
                            rhs=wpart[:, kc * 512:(kc + 1) * 512],
                            start=(kc == 0), stop=False, skip_group_check=True)
                    ins = tensor.matmul(
                        out=pqk[bank][:], lhsT=ones_row_b[:],
                        rhs=bv2r[:, vh * 512:(vh + 1) * 512],
                        start=False, stop=True, skip_group_check=True)
                    c_tn.inc(ins)
                block.tensor(v_mm)
                tn_v = c_tn.n

                def v_ev(scalar, tile=tile, vh=vh, tn_v=tn_v, vx_init_done=vx_init_done):
                    c_tn.wait(scalar, tn_v)
                    if tile == 0 and vh == 0:
                        c_vx.wait(scalar, vx_init_done)
                    bank = tile % 2
                    va = vaug[:, tile * 1040:(tile + 1) * 1040]
                    va = va.rearrange("p (h w) -> p h w", h=16)
                    c_sx.inc(scalar.activation(
                        out=va[:, vh * 8:(vh + 1) * 8, 0:64],
                        in_=pqk[bank].rearrange("p (h w) -> p h w", h=8),
                        func=AF.Copy))
                    pqk_free[bank] = c_sx.n
                block.scalar(v_ev)
            wp_free[0] = c_tn.n
        sx_qkv = c_sx.n

        ctxQ.close()
        if level <= 1:
            ctxC.close()
            return

        ctxX = ExitStack()
        EX = ctxX.enter_context
        kg = EX(nc.sbuf_tensor("kg", [128, DC * 512], BF16))
        vg = EX(nc.sbuf_tensor("vg", [128, 4 * 1040], BF16))
        wpj = EX(nc.sbuf_tensor("wpj", [128, DC * 1024], BF16))
        bpjr = EX(nc.sbuf_tensor("bpjr", [1, D], BF16))
        lcm = EX(nc.sbuf_tensor("lcm", [128, 512], F32))
        bmk = EX(nc.sbuf_tensor("bmk", [128, 64], F32))
        s_cc = Ctr(ctx, nc, "s_cc")
        s_cd = Ctr(ctx, nc, "s_cd")
        s_cg = Ctr(ctx, nc, "s_cg")
        dma_ctrs.extend([s_cc, s_cd, s_cg])

        def x_out(gpsimd):
            c_sx.wait(gpsimd, sx_qkv)
            for dt in range(DC):
                src = k_sb[:, dt * 2048:(dt + 1) * 2048]
                src = src.rearrange("p (b w) -> p b w", w=128)[:, :, 120:128]
                s_cc.inc(gpsimd.dma_start(
                    out=cc_in_k[dt * 128:(dt + 1) * 128, :].rearrange("r (b w) -> r b w", w=8),
                    in_=src), 16)
            for tile in range(NT):
                s_cc.inc(gpsimd.dma_start(
                    out=cc_in_v[tile * 8:(tile + 1) * 8, :],
                    in_=vaug[120:128, tile * 1040:(tile + 1) * 1040]), 16)
            s_cc.wait(gpsimd)
            gpsimd.collective_compute(
                "AllGather", ALU.bypass,
                replica_groups=[[0, 1, 2, 3], [4, 5, 6, 7]],
                ins=[cc_in_k.ap().opt()], outs=[cc_out_k.ap().opt()],
            ).then_inc(s_cg.sem)
            s_cg.n += 1
            gpsimd.collective_compute(
                "AllGather", ALU.bypass,
                replica_groups=[[0, 1, 2, 3], [4, 5, 6, 7]],
                ins=[cc_in_v.ap().opt()], outs=[cc_out_v.ap().opt()],
            ).then_inc(s_cg.sem)
            s_cg.n += 1
        block.gpsimd(x_out)
        cg_done = s_cg.n

        def g_in(sync):
            s_cg.wait(sync, cg_done)
            for r in range(4):
                for dt in range(DC):
                    s_cd.inc(sync.dma_start(
                        out=kg[:, dt * 512 + r * 128: dt * 512 + (r + 1) * 128],
                        in_=cc_out_k[r * 1024 + dt * 128: r * 1024 + (dt + 1) * 128, :]), 16)
                s_cd.inc(sync.dma_start(
                    out=vg[:, r * 1040:(r + 1) * 1040],
                    in_=cc_out_v[r * 128:(r + 1) * 128, :]), 16)
            c_ld.wait(sync)
            c_ld.inc(sync.dma_start(out=lcm[:], in_=lcmask4), 16)
            c_ld.inc(sync.dma_start(out=bmk[:], in_=bmask_in), 16)
        block.sync(g_in)
        ld_g2 = s_cd.n

        def pj_load(gpsimd):
            for dc in range(DC):
                s_cc.inc(gpsimd.dma_start(
                    out=wpj[:, dc * 1024:(dc + 1) * 1024],
                    in_=Wproj[dc * 128:(dc + 1) * 128, :]), 16)
            s_cc.inc(gpsimd.dma_start(out=bpjr[:], in_=bproj_row), 16)
        block.gpsimd(pj_load)
        ld_pj = s_cc.n
        if level <= 2:
            def drain2(sync):
                s_cd.wait(sync, ld_g2)
                s_cc.wait(sync, ld_pj)
            block.sync(drain2)
            ctxX.close()
            ctxC.close()
            return

        ao = EX(nc.sbuf_tensor("ao", [128, 1024], BF16))
        aofm = EX(nc.sbuf_tensor("aofm", [128, DC * 128], BF16))
        ploc_sb = EX(nc.sbuf_tensor("ploc_sb", [128, 1024], BF16))
        pstr_sb = EX(nc.sbuf_tensor("pstr_sb", [128, 1024], BF16))
        rl4 = EX(nc.sbuf_tensor("rl4", [128, 4], F32))
        psl = [EX(nc.psum_tensor(f"psl{i}", [128, 512], F32)) for i in range(2)]
        pss = [EX(nc.psum_tensor(f"pss{i}", [128, 512], F32)) for i in range(2)]
        pav2 = [EX(nc.psum_tensor(f"pavC{i}", [128, 260], F32)) for i in range(2)]
        ptrD = EX(nc.psum_tensor("ptrD", [128, 128], BF16))
        pop = [EX(nc.psum_tensor("pop0", [128, 512], F32))]

        def _pcol(hl):
            return (hl % 2) * 512 + (hl // 2) * 128

        psl_free = [0]
        pss_free = [0]
        pstr_free = [0]
        pav_free2 = [0, 0]
        ao_free = [0]
        aofm_free = [0]
        ptrD_free = [0]
        pop_free = [0]

        import os as _os
        _sub = _os.environ.get("KSUB", "LSAO")
        for slot in range(NT):
            for hp in range(2):
                def loc_mm(tensor, slot=slot, hp=hp):
                    if slot == 0 and hp == 0:
                        s_cc.wait(tensor, ld_pj)
                        s_cd.wait(tensor, ld_g2)
                        c_ld.wait(tensor)
                    c_sx.wait(tensor, psl_free[0])
                    ins = None
                    for hl in range(8):
                        h = hp * 8 + hl
                        dt, base = h // 2, (h % 2) * 64
                        bank = psl[hl % 2]
                        col = (hl // 2) * 128
                        ins = tensor.matmul(
                            out=bank[:, col:col + 128],
                            lhsT=k_sb[base:base + 64, dt * 2048 + slot * 128: dt * 2048 + slot * 128 + 128],
                            rhs=q_sb[base:base + 64, dt * 2048 + slot * 128: dt * 2048 + slot * 128 + 128],
                            start=(hl < 2), stop=(hl >= 6), skip_group_check=True)
                    c_tn.inc(ins)
                block.tensor(loc_mm)
                tn_loc = c_tn.n

                def loc_m(vector, tn_loc=tn_loc):
                    c_tn.wait(vector, tn_loc)
                    vector.tensor_tensor(out=psl[0][:], in0=psl[0][:],
                                         in1=lcm[:], op=ALU.add)
                    c_vx.inc(vector.tensor_tensor(out=psl[1][:], in0=psl[1][:],
                                                  in1=lcm[:], op=ALU.add))
                block.vector(loc_m)
                vx_lm = c_vx.n

                def loc_e(scalar, vx_lm=vx_lm):
                    c_vx.wait(scalar, vx_lm)
                    scalar.activation(out=ploc_sb[:, 0:512], in_=psl[0][:],
                                      func=AF.Exp, scale=inv8[:])
                    c_sx.inc(scalar.activation(out=ploc_sb[:, 512:1024], in_=psl[1][:],
                                               func=AF.Exp, scale=inv8[:]))
                    psl_free[0] = c_sx.n
                block.scalar(loc_e)
                sx_pl = c_sx.n

                do_av = "A" in _sub
                do_str = "S" in _sub
                if do_av:
                    def av_loc(tensor, slot=slot, hp=hp, sx_pl=sx_pl):
                        c_sx.wait(tensor, sx_pl)
                        c_vx.wait(tensor, max(pav_free2))
                        ins = None
                        for b in range(2):
                            for h4 in range(4):
                                hl = b * 4 + h4
                                h = hp * 8 + hl
                                pc = _pcol(hl)
                                va = vaug[:, slot * 1040 + h * 65: slot * 1040 + h * 65 + 65]
                                ins = tensor.matmul(
                                    out=pav2[b][:, h4 * 65:(h4 + 1) * 65],
                                    lhsT=ploc_sb[:, pc:pc + 128],
                                    rhs=va,
                                    start=(hl == 0 or hl == 4), stop=False,
                                    skip_group_check=True)
                        c_tn.inc(ins)
                    block.tensor(av_loc)

                for rt in range(4 if do_str else 0):
                    def str_mm(tensor, slot=slot, hp=hp, rt=rt):
                        c_sx.wait(tensor, pss_free[0])
                        ins = None
                        for hl in range(8):
                            h = hp * 8 + hl
                            dt, base = h // 2, (h % 2) * 64
                            bank = pss[hl % 2]
                            col = (hl // 2) * 128
                            ins = tensor.matmul(
                                out=bank[:, col:col + 128],
                                lhsT=kg[base:base + 64, dt * 512 + rt * 128: dt * 512 + (rt + 1) * 128],
                                rhs=q_sb[base:base + 64, dt * 2048 + slot * 128: dt * 2048 + slot * 128 + 128],
                                start=(hl < 2), stop=(hl >= 6), skip_group_check=True)
                        c_tn.inc(ins)
                    block.tensor(str_mm)
                    tn_str = c_tn.n

                    def str_e(scalar, slot=slot, rt=rt, tn_str=tn_str):
                        c_tn.wait(scalar, tn_str)
                        c_tn.wait(scalar, pstr_free[0])
                        scalar.activation(
                            out=pstr_sb[:, 0:512],
                            in_=pss[0][:], func=AF.Exp, scale=inv8[:],
                            bias=bmk[:, slot * 4 + rt: slot * 4 + rt + 1])
                        c_sx.inc(scalar.activation(
                            out=pstr_sb[:, 512:1024],
                            in_=pss[1][:], func=AF.Exp, scale=inv8[:],
                            bias=bmk[:, slot * 4 + rt: slot * 4 + rt + 1]))
                        pss_free[0] = c_sx.n
                    block.scalar(str_e)
                    sx_se = c_sx.n

                    if do_av:
                        def av_str(tensor, slot=slot, hp=hp, rt=rt, sx_se=sx_se):
                            c_sx.wait(tensor, sx_se)
                            ins = None
                            for b in range(2):
                                for h4 in range(4):
                                    hl = b * 4 + h4
                                    h = hp * 8 + hl
                                    pc = _pcol(hl)
                                    ins = tensor.matmul(
                                        out=pav2[b][:, h4 * 65:(h4 + 1) * 65],
                                        lhsT=pstr_sb[:, pc:pc + 128],
                                        rhs=vg[:, rt * 1040 + h * 65: rt * 1040 + h * 65 + 65],
                                        start=False, stop=(rt == 3 and hl in (3, 7)),
                                        skip_group_check=True)
                            c_tn.inc(ins)
                            pstr_free[0] = c_tn.n
                        block.tensor(av_str)

                if not do_av:
                    continue
                tn_av2 = c_tn.n

                def av_ev(vector, slot=slot, hp=hp, tn_av2=tn_av2):
                    c_tn.wait(vector, tn_av2)
                    if hp == 0:
                        c_tn.wait(vector, ao_free[0])
                    for b in range(2):
                        pv = pav2[b].rearrange("p (h w) -> p h w", h=4)
                        c_vx.inc(vector.reciprocal(out=rl4[:], in_=pv[:, :, 64]))
                        c_vx.wait(vector)
                        for h4 in range(4):
                            h = hp * 8 + b * 4 + h4
                            c_vx.inc(vector.tensor_scalar(
                                out=ao[:, h * 64:(h + 1) * 64],
                                in0=pav2[b][:, h4 * 65: h4 * 65 + 64],
                                scalar1=rl4[:, h4:h4 + 1], scalar2=None, op0=ALU.mult))
                        pav_free2[b] = c_vx.n
                block.vector(av_ev)
            vx_ao = c_vx.n

            if "O" not in _sub:
                continue
            for dc in range(DC):
                def ao_tr(tensor, slot=slot, dc=dc, vx_ao=vx_ao):
                    if dc == 0:
                        c_vx.wait(tensor, vx_ao)
                        c_tn.wait(tensor, aofm_free[0])
                    c_vx.wait(tensor, ptrD_free[0])
                    c_tn.inc(tensor.transpose(
                        out=ptrD[:], in_=ao[:, dc * 128:(dc + 1) * 128],
                        identity=ident[:]))
                block.tensor(ao_tr)
                tn_aot = c_tn.n

                def ao_ev(vector, dc=dc, tn_aot=tn_aot):
                    c_tn.wait(vector, tn_aot)
                    c_vx.inc(vector.tensor_copy(out=aofm[:, dc * 128:(dc + 1) * 128],
                                                in_=ptrD[:]))
                    ptrD_free[0] = c_vx.n
                block.vector(ao_ev)
            vx_aofm = c_vx.n
            ao_free[0] = c_tn.n

            for half in range(2):
                def pj_mm(tensor, slot=slot, half=half, vx_aofm=vx_aofm):
                    if half == 0:
                        c_vx.wait(tensor, vx_aofm)
                    bank = 0
                    c_vx.wait(tensor, pop_free[bank])
                    for dc in range(DC):
                        tensor.matmul(
                            out=pop[bank][:],
                            lhsT=aofm[:, dc * 128:(dc + 1) * 128],
                            rhs=wpj[:, dc * 1024 + half * 512: dc * 1024 + (half + 1) * 512],
                            start=(dc == 0), stop=False, skip_group_check=True)
                    ins = tensor.matmul(
                        out=pop[bank][:], lhsT=ones_row_b[:],
                        rhs=bpjr[:, half * 512:(half + 1) * 512],
                        start=False, stop=True, skip_group_check=True)
                    c_tn.inc(ins)
                block.tensor(pj_mm)
                tn_pj = c_tn.n
                if half == 1:
                    aofm_free[0] = c_tn.n

                def pj_add(vector, slot=slot, half=half, tn_pj=tn_pj):
                    c_tn.wait(vector, tn_pj)
                    bank = 0
                    sl = resid[:, slot * 1024 + half * 512: slot * 1024 + (half + 1) * 512]
                    c_vx.inc(vector.tensor_tensor(out=sl, in0=sl, in1=pop[bank][:],
                                                  op=ALU.add))
                    pop_free[bank] = c_vx.n
                block.vector(pj_add)

        ctxX.close()
        ctxC.close()

    wp_free = [0]

    import os
    _ph = os.environ.get("KPHASES", "ABCD")
    if "B" in _ph:
        emit_ffn("B")
        barrier()
    if "C" in _ph:
        emit_attn_b(level=int(os.environ.get("KLEVEL", "3")))
        barrier()
    if "D" in _ph:
        emit_ffn("D")
        barrier()

    def store_out(sync):
        c_vx.wait(sync)
        c_sx.wait(sync)
        c_tn.wait(sync)
        c_st.inc(sync.dma_start(
            out=out.rearrange("(t p) d -> p t d", p=128),
            in_=resid.rearrange("p (t d) -> p t d", t=NT)), 16)
        c_st.wait(sync)
    block.sync(store_out)

    ctx.close()
    return nc


def _bmask(core):
    seq = core // 4
    m = np.zeros((128, 64), dtype=np.float32)
    my_tiles = _assignment(core)
    for slot in range(NT):
        a_self = my_tiles[slot]
        for rt in range(4):
            rank_core = seq * 4 + rt
            r_tiles = _assignment(rank_core)
            for p in range(128):
                a_key = r_tiles[p // 8]
                if not (a_key < a_self):
                    m[p, slot * 4 + rt] = np.float32(-1e38)
    return np.ascontiguousarray(m)


def _host_inputs(core, inputs):
    import ml_dtypes
    x = np.asarray(inputs["L0_attn_a_x"])
    seq = core // 4
    tiles = _kv_perm_tiles(core)
    rows = np.concatenate([np.arange(128 * t, 128 * t + 128) for t in tiles])
    xp = x[seq][rows]
    xkv_cm = np.ascontiguousarray(xp.T)

    slots = _assignment(core)
    own_rows = np.concatenate([np.arange(128 * a, 128 * a + 128) for a in slots])
    xown = np.ascontiguousarray(x[seq][own_rows])

    k = core % 4
    am = np.zeros((CH, 1024, 256), dtype=np.float32)
    for c in range(CH):
        own = [8 * c + 2 * k, 8 * c + 2 * k + 1]
        rest = [8 * c + j for j in range(8) if 8 * c + j not in own]
        order = own + rest
        key_rows = np.concatenate([np.arange(128 * t, 128 * t + 128) for t in order])
        q_rows = np.concatenate([np.arange(128 * t, 128 * t + 128) for t in own])
        am[c] = np.where(key_rows[:, None] > q_rows[None, :], -10000.0, 0.0)

    return {
        "xkv_cm": xkv_cm,
        "xown": xown,
        "amask": am,
        "Wq": np.asarray(inputs["L0_attn_a_Wq"]),
        "Wk": np.asarray(inputs["L0_attn_a_Wk"]),
        "Wv": np.asarray(inputs["L0_attn_a_Wv"]),
        "Wo": np.asarray(inputs["L0_attn_a_Wo"]),
        "bq_pt": np.ascontiguousarray(np.asarray(inputs["L0_attn_a_bq"]).reshape(8, 128).T),
        "bk_pt": np.ascontiguousarray(np.asarray(inputs["L0_attn_a_bk"]).reshape(8, 128).T),
        "bv_row": np.ascontiguousarray(np.asarray(inputs["L0_attn_a_bv"]).reshape(1, D)),
        "bo_row": np.ascontiguousarray(np.asarray(inputs["L0_attn_a_bo"]).reshape(1, D)),
        "ident_bf": np.eye(128, dtype=ml_dtypes.bfloat16),
        "Wg": np.asarray(inputs["L0_ffn_Wg"]),
        "bg_bc": np.ascontiguousarray(np.broadcast_to(np.asarray(inputs["L0_ffn_bg"]), (128, 8))),
        "W0": np.asarray(inputs["L0_ffn_W0"]),
        "W1": np.asarray(inputs["L0_ffn_W1"]),
        "W2": np.asarray(inputs["L0_ffn_W2"]),
        "b0_pt": np.ascontiguousarray(np.asarray(inputs["L0_ffn_b0"]).reshape(8, 32, 128).transpose(2, 0, 1).reshape(128, 256)),
        "b1_pt": np.ascontiguousarray(np.asarray(inputs["L0_ffn_b1"]).reshape(8, 32, 128).transpose(2, 0, 1).reshape(128, 256)),
        "b2_in": np.asarray(inputs["L0_ffn_b2"]),
        "Wattn": np.asarray(inputs["L0_attn_b_W_attn"]),
        "battn_pt": np.ascontiguousarray(np.asarray(inputs["L0_attn_b_b_attn"])[0:2048].reshape(16, 128).T),
        "battnv_row": np.ascontiguousarray(np.asarray(inputs["L0_attn_b_b_attn"])[2048:3072].reshape(1, D)),
        "Wproj": np.asarray(inputs["L0_attn_b_W_proj"]),
        "bproj_row": np.ascontiguousarray(np.asarray(inputs["L0_attn_b_b_proj"]).reshape(1, D)),
        "lcmask4": np.ascontiguousarray(np.tile(
            np.where(np.arange(128)[:, None] > np.arange(128)[None, :],
                     np.float32(-1e38), np.float32(0.0)), (1, 4))),
        "bmask_in": _bmask(core),
    }


def kernel(**inputs):
    nc = bass.Bass("TRN2", target_bir_lowering=False,
                   dynamic_dma_scratch_size=16384)
    build(nc)
    in_maps = [_host_inputs(core, inputs) for core in range(NCORE)]
    res = run_bass_kernel_spmd(nc, in_maps, core_ids=list(range(NCORE)))

    full = np.zeros((B, S, D), dtype=np.float32)
    for core in range(NCORE):
        seq = core // 4
        o = np.asarray(res.results[core]["out"])
        for s, a in enumerate(_assignment(core)):
            full[seq, 128 * a:128 * a + 128] = o[128 * s:128 * s + 128]
    return full


# version 07
# speedup vs baseline: 1.0314x; 1.0314x over previous
import numpy as np
from contextlib import ExitStack

import concourse.bass as bass
import concourse.mybir as mybir
from concourse.bass_utils import run_bass_kernel_spmd

F32 = mybir.dt.float32
BF16 = mybir.dt.bfloat16
AF = mybir.ActivationFunctionType
ALU = mybir.AluOpType
AX = mybir.AxisListType

D = 1024
S = 8192
B = 2
NCORE = 8
NT = 16
CH = 8
DC = 8
C_SHIFT = 16.0
NEXP = 8
D_HID = 4096

CHUNK_OF_SLOT = [7 - s // 2 for s in range(NT)]


def _assignment(core):
    k = core % 4
    return [8 * CHUNK_OF_SLOT[s] + 2 * k + (s % 2) for s in range(NT)]


def _kv_perm_tiles(core):
    k = core % 4
    tiles = []
    for c in range(CH):
        own = [8 * c + 2 * k, 8 * c + 2 * k + 1]
        rest = [8 * c + j for j in range(8) if 8 * c + j not in own]
        tiles.extend(own + rest)
    return tiles


class Ctr:
    def __init__(self, ctx, nc, name):
        self.sem = ctx.enter_context(nc.semaphore(name))
        self.n = 0

    def inc(self, inst, k=1):
        inst.then_inc(self.sem, k)
        self.n += k
        return inst

    def wait(self, eng, val=None):
        v = self.n if val is None else val
        if v > 0:
            eng.wait_ge(self.sem, v)


def build(nc):
    ctx = ExitStack()

    def din(name, shape, dtype=F32):
        return nc.dram_tensor(name, shape, dtype, kind="ExternalInput").ap()

    xkv_cm = din("xkv_cm", [D, S])
    xown = din("xown", [NT * 128, D])
    amask = din("amask", [CH, 1024, 256])
    Wq = din("Wq", [D, D])
    Wk = din("Wk", [D, D])
    Wv = din("Wv", [D, D])
    Wo = din("Wo", [D, D])
    bq_pt = din("bq_pt", [128, 8])
    bk_pt = din("bk_pt", [128, 8])
    bv_row = din("bv_row", [1, D])
    bo_row = din("bo_row", [1, D])
    ident_in = din("ident_bf", [128, 128], BF16)
    Wg = din("Wg", [D, 8])
    bg_bc = din("bg_bc", [128, 8])
    W0 = din("W0", [8, D, 4096])
    W1 = din("W1", [8, D, 4096])
    W2 = din("W2", [8, 4096, D])
    b0_pt = din("b0_pt", [128, 256])
    b1_pt = din("b1_pt", [128, 256])
    b2_in = din("b2_in", [8, D])
    Wattn = din("Wattn", [D, 3 * D])
    battn_pt = din("battn_pt", [128, 16])
    battnv_row = din("battnv_row", [1, D])
    Wproj = din("Wproj", [D, D])
    bproj_row = din("bproj_row", [1, D])
    lcmask4 = din("lcmask4", [128, 512])
    bmask_in = din("bmask_in", [128, 64])

    out = nc.dram_tensor("out", [NT * 128, D], F32, kind="ExternalOutput").ap()

    k_scr = nc.dram_tensor("k_scr", [CH, 128, DC * 1024], BF16, kind="Internal").ap()
    v_scr = nc.dram_tensor("v_scr", [CH, 128, 8 * 1024], BF16, kind="Internal").ap()
    q_scr = nc.dram_tensor("q_scr", [128, DC, 2048], BF16, kind="Internal").ap()

    E = ctx.enter_context
    ident = E(nc.sbuf_tensor("ident", [128, 128], BF16))
    ones_col = E(nc.sbuf_tensor("ones_col", [128, 1], BF16))
    ones_row_f = E(nc.sbuf_tensor("ones_row_f", [1, 128], F32))
    ones_row_b = E(nc.sbuf_tensor("ones_row_b", [1, 128], BF16))
    negC = E(nc.sbuf_tensor("negC", [128, 1], F32))
    inv32 = E(nc.sbuf_tensor("inv32", [128, 1], F32))
    epsc = E(nc.sbuf_tensor("epsc", [128, 1], F32))
    eps1 = E(nc.sbuf_tensor("eps1", [1, 1], F32))

    block = E(nc.Block())

    c_ld = Ctr(ctx, nc, "c_ld")
    c_lw = Ctr(ctx, nc, "c_lw")
    c_vx = Ctr(ctx, nc, "c_vx")
    c_sx = Ctr(ctx, nc, "c_sx")
    c_tn = Ctr(ctx, nc, "c_tn")
    c_st = Ctr(ctx, nc, "c_st")
    c_bar = Ctr(ctx, nc, "c_bar")
    s_x = [Ctr(ctx, nc, f"s_x{i}") for i in range(2)]
    s_kv = [Ctr(ctx, nc, f"s_kv{i}") for i in range(2)]
    s_qp = Ctr(ctx, nc, "s_qp")
    s_am = Ctr(ctx, nc, "s_am")
    dma_ctrs = [c_st, c_ld, c_lw, s_qp, s_am] + s_x + s_kv

    ENGS = ["sync", "gpsimd", "tensor", "vector", "scalar"]

    def barrier():
        def q(sync):
            for ctr in dma_ctrs:
                ctr.wait(sync)
        block.sync(q)
        target = c_bar.n + len(ENGS)
        for name in ENGS:
            def f(eng, target=target):
                eng.sem_inc(c_bar.sem, 1)
                eng.wait_ge(c_bar.sem, target)
            getattr(block, name)(f)
        c_bar.n = target

    def init_vec(vector):
        vector.memset(ones_col[:], 1.0)
        vector.memset(ones_row_f[:], 1.0)
        vector.memset(ones_row_b[:], 1.0)
        vector.memset(negC[:], -C_SHIFT)
        vector.memset(inv32[:], 1.0 / 32.0)
        vector.memset(epsc[:], 1e-5)
        c_vx.inc(vector.memset(eps1[:], 1e-5))
    block.vector(init_vec)

    def init_sync(sync):
        c_ld.inc(sync.dma_start(out=ident[:], in_=ident_in), 16)
    block.sync(init_sync)
    ld_init = c_ld.n

    ctxA0 = ExitStack()
    EA = ctxA0.enter_context
    xstg = [EA(nc.sbuf_tensor(f"xstg{b}", [128, DC * 512], F32)) for b in range(2)]
    xlnb = [EA(nc.sbuf_tensor(f"xlnb{b}", [128, DC * 512], BF16)) for b in range(2)]
    xbb = EA(nc.sbuf_tensor("xbb", [128, 2 * 512], BF16))
    x2b = EA(nc.sbuf_tensor("x2b", [128, 2 * 512], BF16))
    tmpf = EA(nc.sbuf_tensor("tmpf", [128, 512], F32))
    mean_r = EA(nc.sbuf_tensor("mean_r", [1, 512], F32))
    t_r = EA(nc.sbuf_tensor("t_r", [1, 512], F32))
    msq_r = EA(nc.sbuf_tensor("msq_r", [1, 512], F32))
    var_r = EA(nc.sbuf_tensor("var_r", [1, 512], F32))
    std_r = EA(nc.sbuf_tensor("std_r", [1, 512], F32))
    rstd_r = EA(nc.sbuf_tensor("rstd_r", [1, 512], F32))
    Wq_sb = EA(nc.sbuf_tensor("Wq_sb", [128, DC * 1024], BF16))
    Wk_sb = EA(nc.sbuf_tensor("Wk_sb", [128, DC * 1024], BF16))
    Wv_sb = EA(nc.sbuf_tensor("Wv_sb", [128, DC * 1024], BF16))
    bqp = EA(nc.sbuf_tensor("bqp", [128, 8], F32))
    bkp = EA(nc.sbuf_tensor("bkp", [128, 8], F32))
    bvr = EA(nc.sbuf_tensor("bvr", [1, D], BF16))
    K_sb = EA(nc.sbuf_tensor("K_sb", [128, DC * 1024], BF16))
    V_sb = EA(nc.sbuf_tensor("V_sb", [128, 8 * 1024], BF16))
    Qc_sb = EA(nc.sbuf_tensor("Qc_sb", [128, DC * 256], BF16))

    pst1 = EA(nc.psum_tensor("pst1", [1, 512], F32))
    pst2 = EA(nc.psum_tensor("pst2", [1, 512], F32))
    pbcm = EA(nc.psum_tensor("pbcm", [128, 512], F32))
    pbcr = EA(nc.psum_tensor("pbcr", [128, 512], F32))
    pmm = [EA(nc.psum_tensor(f"pmmA{i}", [128, 512], F32)) for i in range(4)]

    def a0_weights(gpsimd):
        for W, sb in ((Wq, Wq_sb), (Wk, Wk_sb), (Wv, Wv_sb)):
            for kc in range(DC):
                c_lw.inc(gpsimd.dma_start(
                    out=sb[:, kc * 1024:(kc + 1) * 1024],
                    in_=W[kc * 128:(kc + 1) * 128, :]), 16)
        c_lw.inc(gpsimd.dma_start(out=bvr[:], in_=bv_row), 16)
    block.gpsimd(a0_weights)
    lw_a0 = c_lw.n

    def a0_bias(sync):
        c_ld.inc(sync.dma_start(out=bqp[:], in_=bq_pt), 16)
        c_ld.inc(sync.dma_start(out=bkp[:], in_=bk_pt), 16)
    block.sync(a0_bias)
    ld_a0 = c_ld.n

    xstg_free = [(0, 0)] * 2
    xlnb_free = [0] * 2
    xbb_free = [0] * 2
    x2b_free = [0] * 2
    pst_free = [0]
    pmm_free = [0] * 4
    sx_evac = 0
    st_kv = 0

    def a0_prep(c, h, sb):
        global_state = None

        def stage(sync):
            vfree, sfree = xstg_free[sb]
            c_vx.wait(sync, vfree)
            c_sx.wait(sync, sfree)
            src = xkv_cm[:, 1024 * c + 512 * h: 1024 * c + 512 * (h + 1)]
            src = src.rearrange("(dc p) n -> p dc n", p=128)
            s_x[sb].inc(sync.dma_start(
                out=xstg[sb].rearrange("p (dc n) -> p dc n", dc=DC), in_=src), 16)
        block.sync(stage)
        ld_here = s_x[sb].n

        vx_cast = []
        sx_sq = []
        for dc in range(DC):
            def cast1(vector, dc=dc):
                if dc == 0:
                    s_x[sb].wait(vector, ld_here)
                c_tn.wait(vector, xbb_free[dc % 2])
                c_vx.inc(vector.tensor_copy(
                    out=xbb[:, (dc % 2) * 512:(dc % 2 + 1) * 512],
                    in_=xstg[sb][:, dc * 512:(dc + 1) * 512]))
                vx_cast.append(c_vx.n)

            def sq1(scalar, dc=dc):
                if dc == 0:
                    s_x[sb].wait(scalar, ld_here)
                c_tn.wait(scalar, x2b_free[dc % 2])
                c_sx.inc(scalar.activation(
                    out=x2b[:, (dc % 2) * 512:(dc % 2 + 1) * 512],
                    in_=xstg[sb][:, dc * 512:(dc + 1) * 512], func=AF.Square))
                sx_sq.append(c_sx.n)

            def mm1(tensor, dc=dc):
                if dc == 0:
                    c_sx.wait(tensor, pst_free[0])
                c_vx.wait(tensor, vx_cast[dc])
                ins = tensor.matmul(out=pst1[:], lhsT=ones_col[:],
                                    rhs=xbb[:, (dc % 2) * 512:(dc % 2 + 1) * 512],
                                    start=(dc == 0), stop=(dc == DC - 1),
                                    skip_group_check=True)
                c_tn.inc(ins)
                xbb_free[dc % 2] = c_tn.n
                c_sx.wait(tensor, sx_sq[dc])
                ins = tensor.matmul(out=pst2[:], lhsT=ones_col[:],
                                    rhs=x2b[:, (dc % 2) * 512:(dc % 2 + 1) * 512],
                                    start=(dc == 0), stop=(dc == DC - 1),
                                    skip_group_check=True)
                c_tn.inc(ins)
                x2b_free[dc % 2] = c_tn.n
            block.vector(cast1)
            block.scalar(sq1)
            block.tensor(mm1)
        tn_stats = c_tn.n

        def rowstats_s(scalar):
            c_tn.wait(scalar, tn_stats)
            c_sx.inc(scalar.mul(out=mean_r[:], in_=pst1[:], mul=1.0 / D))
            scalar.mul(out=t_r[:], in_=pst2[:], mul=1.0 / D)
            c_sx.wait(scalar)
            c_sx.inc(scalar.activation(out=msq_r[:], in_=mean_r[:], func=AF.Square))
        block.scalar(rowstats_s)
        pst_free[0] = c_sx.n
        sx_rows = c_sx.n

        def rowstats_v(vector):
            c_sx.wait(vector, sx_rows)
            c_vx.inc(vector.tensor_tensor(out=var_r[:], in0=t_r[:], in1=msq_r[:],
                                          op=ALU.subtract))
        block.vector(rowstats_v)
        vx_var = c_vx.n

        def rowstats_s2(scalar):
            c_vx.wait(scalar, vx_var)
            c_sx.inc(scalar.activation(out=std_r[:], in_=var_r[:], func=AF.Sqrt,
                                       bias=eps1[:]))
        block.scalar(rowstats_s2)
        sx_std = c_sx.n

        def rowstats_v2(vector):
            c_sx.wait(vector, sx_std)
            c_vx.inc(vector.reciprocal(out=rstd_r[:], in_=std_r[:]))
        block.vector(rowstats_v2)
        vx_rstd = c_vx.n

        def bcast(tensor):
            c_vx.wait(tensor, vx_rstd)
            tensor.matmul(out=pbcm[:], lhsT=ones_row_f[:], rhs=mean_r[:],
                          start=True, stop=True, skip_group_check=True)
            c_tn.inc(tensor.matmul(out=pbcr[:], lhsT=ones_row_f[:], rhs=rstd_r[:],
                                   start=True, stop=True, skip_group_check=True))
        block.tensor(bcast)
        tn_bc = c_tn.n

        def apply_ln(vector):
            c_tn.wait(vector, tn_bc)
            c_tn.wait(vector, xlnb_free[sb])
            for dc in range(DC):
                if dc > 0:
                    c_vx.wait(vector)
                c_vx.inc(vector.tensor_tensor(
                    out=tmpf[:],
                    in0=xstg[sb][:, dc * 512:(dc + 1) * 512],
                    in1=pbcm[:], op=ALU.subtract))
                c_vx.wait(vector)
                c_vx.inc(vector.tensor_tensor(
                    out=xlnb[sb][:, dc * 512:(dc + 1) * 512],
                    in0=tmpf[:], in1=pbcr[:], op=ALU.mult))
        block.vector(apply_ln)
        xstg_free[sb] = (c_vx.n, sx_sq[-1])
        return c_vx.n

    def a0_proj(c, h, sb, vx_ln):
        jobs = [("k", dk) for dk in range(DC)]
        jobs += [("v", (tt, vh)) for tt in range(4) for vh in range(2)]
        if h == 0:
            jobs += [("q", dq) for dq in range(DC)]

        first_proj = True
        for kind, idx in jobs:
            def proj(tensor, kind=kind, idx=idx, first=first_proj):
                if first:
                    c_vx.wait(tensor, vx_ln)
                    c_lw.wait(tensor, lw_a0)
                    c_ld.wait(tensor, ld_a0)
                    c_st.wait(tensor, st_state[0])
                if kind == "k":
                    dk = idx
                    bank = dk % 2
                    c_sx.wait(tensor, pmm_free[bank])
                    for kc in range(DC):
                        ins = tensor.matmul(
                            out=pmm[bank][:],
                            lhsT=Wk_sb[:, kc * 1024 + dk * 128: kc * 1024 + dk * 128 + 128],
                            rhs=xlnb[sb][:, kc * 512:(kc + 1) * 512],
                            start=(kc == 0), stop=(kc == DC - 1), skip_group_check=True)
                    c_tn.inc(ins)
                elif kind == "v":
                    tt, vh = idx
                    bank = 2 + (tt * 2 + vh) % 2
                    c_sx.wait(tensor, pmm_free[bank])
                    for kc in range(DC):
                        tensor.matmul(
                            out=pmm[bank][:],
                            lhsT=xlnb[sb][:, kc * 512 + tt * 128: kc * 512 + tt * 128 + 128],
                            rhs=Wv_sb[:, kc * 1024 + vh * 512: kc * 1024 + (vh + 1) * 512],
                            start=(kc == 0), stop=False, skip_group_check=True)
                    ins = tensor.matmul(
                        out=pmm[bank][:], lhsT=ones_row_b[:],
                        rhs=bvr[:, vh * 512:(vh + 1) * 512],
                        start=False, stop=True, skip_group_check=True)
                    c_tn.inc(ins)
                else:
                    dq = idx
                    bank = dq % 2
                    c_sx.wait(tensor, pmm_free[bank])
                    for kc in range(DC):
                        ins = tensor.matmul(
                            out=pmm[bank][:, 0:256],
                            lhsT=Wq_sb[:, kc * 1024 + dq * 128: kc * 1024 + dq * 128 + 128],
                            rhs=xlnb[sb][:, kc * 512: kc * 512 + 256],
                            start=(kc == 0), stop=(kc == DC - 1), skip_group_check=True)
                    c_tn.inc(ins)
            block.tensor(proj)
            first_proj = False
            tn_here = c_tn.n

            def evac1(scalar, kind=kind, idx=idx, h=h, tn_here=tn_here):
                c_tn.wait(scalar, tn_here)
                if kind == "k":
                    dk = idx
                    bank = dk % 2
                    ins = scalar.activation(
                        out=K_sb[:, dk * 1024 + h * 512: dk * 1024 + h * 512 + 512],
                        in_=pmm[bank][:], func=AF.Relu, bias=bkp[:, dk:dk + 1])
                elif kind == "v":
                    tt, vh = idx
                    bank = 2 + (idx[0] * 2 + idx[1]) % 2
                    tile = 4 * h + tt
                    ins = scalar.activation(
                        out=V_sb[:, tile * 1024 + vh * 512: tile * 1024 + vh * 512 + 512],
                        in_=pmm[bank][:], func=AF.Relu)
                else:
                    dq = idx
                    bank = dq % 2
                    ins = scalar.activation(
                        out=Qc_sb[:, dq * 256:(dq + 1) * 256],
                        in_=pmm[bank][:, 0:256], func=AF.Relu, bias=bqp[:, dq:dq + 1])
                c_sx.inc(ins)
                bank = (idx % 2) if kind in ("k", "q") else 2 + (idx[0] * 2 + idx[1]) % 2
                pmm_free[bank] = c_sx.n
            block.scalar(evac1)
        xlnb_free[sb] = c_tn.n
        return c_sx.n

    st_state = [0]
    halves = [(c, h) for c in range(CH) for h in range(2)]
    vx_lns = {}
    vx_lns[0] = a0_prep(*halves[0], 0)
    for i, (c, h) in enumerate(halves):
        if i + 1 < len(halves):
            c1, h1 = halves[i + 1]
            vx_lns[i + 1] = a0_prep(c1, h1, (i + 1) % 2)
        sx_evac = a0_proj(c, h, i % 2, vx_lns[i])
        if h == 1:
            def kv_store(sync, c=c, sx_evac=sx_evac):
                c_sx.wait(sync, sx_evac)
                c_st.inc(sync.dma_start(out=k_scr[c], in_=K_sb[:]), 16)
                c_st.inc(sync.dma_start(out=v_scr[c], in_=V_sb[:]), 16)
                dst = q_scr[:, :, 256 * (7 - c): 256 * (8 - c)]
                c_st.inc(sync.dma_start(
                    out=dst, in_=Qc_sb.rearrange("p (dq n) -> p dq n", dq=DC)), 16)
            block.sync(kv_store)
            st_state[0] = c_st.n

    ctxA0.close()
    barrier()

    resid = E(nc.sbuf_tensor("resid", [128, NT * 1024], F32))

    ctxA1 = ExitStack()
    EA1 = ctxA1.enter_context
    Kbuf = [EA1(nc.sbuf_tensor(f"Kbuf{b}", [128, DC * 1024], BF16)) for b in range(2)]
    Vbuf = [EA1(nc.sbuf_tensor(f"Vbuf{b}", [128, 8 * 1024], BF16)) for b in range(2)]
    Qp_sb = EA1(nc.sbuf_tensor("Qp_sb", [128, DC * 256], BF16))
    P_sb = EA1(nc.sbuf_tensor("P_sb", [128, 8 * 256], BF16))
    am_sb = EA1(nc.sbuf_tensor("am_sb", [128, 8 * 256], F32))
    Wo_sb = EA1(nc.sbuf_tensor("Wo_sb", [128, DC * 1024], BF16))
    bor = EA1(nc.sbuf_tensor("bor", [1, D], BF16))
    lsb = EA1(nc.sbuf_tensor("lsb", [128, 2], F32))
    rl = EA1(nc.sbuf_tensor("rl", [128, 2], F32))
    ybf = EA1(nc.sbuf_tensor("ybf", [128, 2 * 1024], BF16))
    yfm = EA1(nc.sbuf_tensor("yfm", [128, DC * 256], BF16))
    orel = EA1(nc.sbuf_tensor("orel", [128, 512], F32))
    xo_sb = EA1(nc.sbuf_tensor("xo_sb", [128, 2 * 1024], F32))

    pl = EA1(nc.psum_tensor("pl", [128, 2], F32))
    pav = [EA1(nc.psum_tensor(f"pav{i}", [128, 512], F32)) for i in range(4)]
    ps = [EA1(nc.psum_tensor(f"psA1{i}", [128, 256], F32)) for i in range(2)]
    ptr = EA1(nc.psum_tensor("ptrA1", [128, 256], BF16))

    def a1_weights(gpsimd):
        c_lw.wait(gpsimd)
        for dc in range(DC):
            c_lw.inc(gpsimd.dma_start(
                out=Wo_sb[:, dc * 1024:(dc + 1) * 1024],
                in_=Wo[dc * 128:(dc + 1) * 128, :]), 16)
        c_lw.inc(gpsimd.dma_start(out=bor[:], in_=bo_row), 16)
    block.gpsimd(a1_weights)
    lw_a1 = c_lw.n

    s_kvg = [Ctr(ctx, nc, f"s_kvg{i}") for i in range(2)]
    dma_ctrs += s_kvg

    st_a1_guard = c_st.n
    kbuf_free = [0] * 2
    vbuf_free = [0] * 2
    ps_free = [0] * 2
    am_free = 0
    qp_free = 0
    xo_free = 0
    pl_free = 0
    pav_free = [0] * 4
    ybf_free = 0
    rl_free = 0
    ptr_free = 0
    orel_free = 0
    first_sync_a1 = True

    for pair in range(CH):
        n_ch = 8 - pair

        def q_load(sync, pair=pair):
            nonlocal first_sync_a1
            if first_sync_a1:
                c_st.wait(sync, st_a1_guard)
                first_sync_a1 = False
            c_tn.wait(sync, qp_free)
            src = q_scr[:, :, 256 * pair:256 * (pair + 1)]
            s_qp.inc(sync.dma_start(
                out=Qp_sb.rearrange("p (dq n) -> p dq n", dq=DC), in_=src), 16)
            c_vx.wait(sync, xo_free)
            s_qp.inc(sync.dma_start(
                out=xo_sb.rearrange("p (t d) -> p t d", t=2),
                in_=xown.rearrange("(t p) d -> p t d", p=128)[:, 2 * pair:2 * pair + 2]), 16)
        block.sync(q_load)
        ld_q = s_qp.n

        def m_load(sync, pair=pair):
            c_vx.wait(sync, am_free)
            src = amask[7 - pair].rearrange("(kt p) n -> p kt n", p=128)
            s_am.inc(sync.dma_start(
                out=am_sb.rearrange("p (kt n) -> p kt n", kt=8), in_=src), 16)
        block.sync(m_load)
        ld_m = s_am.n

        for cv in range(n_ch):
            sbuf = cv % 2
            diag = (cv == n_ch - 1)

            def kv_load(sync, cv=cv, sbuf=sbuf):
                c_tn.wait(sync, kbuf_free[sbuf])
                s_kv[sbuf].inc(sync.dma_start(out=Kbuf[sbuf][:], in_=k_scr[cv]), 16)
            block.sync(kv_load)

            def v_load(gpsimd, cv=cv, sbuf=sbuf):
                if cv == 0 and pair == 0:
                    c_st.wait(gpsimd, st_a1_guard)
                c_tn.wait(gpsimd, vbuf_free[sbuf])
                s_kvg[sbuf].inc(gpsimd.dma_start(out=Vbuf[sbuf][:], in_=v_scr[cv]), 16)
            block.gpsimd(v_load)
            ld_kv = s_kv[sbuf].n
            ld_v = s_kvg[sbuf].n

            p_ready = []
            for kt in range(8):
                def score(tensor, kt=kt, sbuf=sbuf, ld_kv=ld_kv, ld_q=ld_q):
                    if kt == 0:
                        s_kv[sbuf].wait(tensor, ld_kv)
                        s_qp.wait(tensor, ld_q)
                    bank = kt % 2
                    c_sx.wait(tensor, ps_free[bank])
                    for dc in range(DC):
                        ins = tensor.matmul(
                            out=ps[bank][:],
                            lhsT=Kbuf[sbuf][:, dc * 1024 + kt * 128: dc * 1024 + kt * 128 + 128],
                            rhs=Qp_sb[:, dc * 256:(dc + 1) * 256],
                            start=(dc == 0), stop=(dc == DC - 1), skip_group_check=True)
                    c_tn.inc(ins)
                block.tensor(score)
                tn_sc = c_tn.n

                vx_m = 0
                if diag:
                    def maskadd(vector, kt=kt, tn_sc=tn_sc, ld_m=ld_m):
                        c_tn.wait(vector, tn_sc)
                        s_am.wait(vector, ld_m)
                        bank = kt % 2
                        c_vx.inc(vector.tensor_tensor(
                            out=ps[bank][:], in0=ps[bank][:],
                            in1=am_sb[:, kt * 256:(kt + 1) * 256], op=ALU.add))
                    block.vector(maskadd)
                    vx_m = c_vx.n

                def expev(scalar, kt=kt, diag=diag, tn_sc=tn_sc, vx_m=vx_m):
                    bank = kt % 2
                    if diag:
                        c_vx.wait(scalar, vx_m)
                    else:
                        c_tn.wait(scalar, tn_sc)
                    c_sx.inc(scalar.activation(
                        out=P_sb[:, kt * 256:(kt + 1) * 256], in_=ps[bank][:],
                        func=AF.Exp, bias=negC[:], scale=inv32[:]))
                    ps_free[bank] = c_sx.n
                block.scalar(expev)
                p_ready.append(c_sx.n)
            if diag:
                am_free = c_vx.n

            def av(tensor, cv=cv, sbuf=sbuf, n_ch=n_ch, p_ready=p_ready,
                   ld_v=ld_v):
                s_kvg[sbuf].wait(tensor, ld_v)
                if cv == 0:
                    c_vx.wait(tensor, pl_free)
                    c_sx.wait(tensor, max(pav_free))
                ins = None
                for qt in range(2):
                    for kt in range(8):
                        c_sx.wait(tensor, p_ready[kt])
                        tensor.matmul(
                            out=pl[:, qt:qt + 1],
                            lhsT=P_sb[:, kt * 256 + qt * 128: kt * 256 + qt * 128 + 128],
                            rhs=ones_col[:],
                            start=(cv == 0 and kt == 0 and qt == 0),
                            stop=(cv == n_ch - 1 and kt == 7),
                            skip_group_check=True)
                        for half in range(2):
                            ins = tensor.matmul(
                                out=pav[qt * 2 + half][:],
                                lhsT=P_sb[:, kt * 256 + qt * 128: kt * 256 + qt * 128 + 128],
                                rhs=Vbuf[sbuf][:, kt * 1024 + half * 512: kt * 1024 + (half + 1) * 512],
                                start=(cv == 0 and kt == 0),
                                stop=(cv == n_ch - 1 and kt == 7),
                                skip_group_check=True)
                c_tn.inc(ins)
                kbuf_free[sbuf] = c_tn.n
                vbuf_free[sbuf] = c_tn.n
            block.tensor(av)
        tn_av = c_tn.n
        qp_free_next = c_tn.n

        def fin_v(vector):
            c_tn.wait(vector, tn_av)
            c_sx.wait(vector, rl_free)
            c_vx.inc(vector.tensor_copy(out=lsb[:], in_=pl[:]))
            c_vx.wait(vector)
            c_vx.inc(vector.reciprocal(out=rl[:], in_=lsb[:]))
        block.vector(fin_v)
        pl_free = c_vx.n
        vx_rl = c_vx.n

        def fin_s(scalar):
            c_vx.wait(scalar, vx_rl)
            c_tn.wait(scalar, ybf_free)
            for qt in range(2):
                for half in range(2):
                    c_sx.inc(scalar.activation(
                        out=ybf[:, qt * 1024 + half * 512: qt * 1024 + (half + 1) * 512],
                        in_=pav[qt * 2 + half][:], func=AF.Copy, scale=rl[:, qt:qt + 1]))
        block.scalar(fin_s)
        sx_ybf = c_sx.n
        rl_free = c_sx.n
        for i in range(4):
            pav_free[i] = sx_ybf

        tr_done = []
        tr_evac = []
        for dc in range(DC):
            def tr(tensor, dc=dc):
                if dc == 0:
                    c_sx.wait(tensor, sx_ybf)
                    c_ld.wait(tensor)
                    c_vx.wait(tensor, ptr_free)
                else:
                    c_vx.wait(tensor, tr_evac[dc - 1])
                ins = None
                for qt in range(2):
                    ins = tensor.transpose(
                        out=ptr[:, qt * 128:(qt + 1) * 128],
                        in_=ybf[:, qt * 1024 + dc * 128: qt * 1024 + dc * 128 + 128],
                        identity=ident[:])
                c_tn.inc(ins)
                tr_done.append(c_tn.n)

            def trev(vector, dc=dc):
                c_tn.wait(vector, tr_done[dc])
                c_vx.inc(vector.tensor_copy(out=yfm[:, dc * 256:(dc + 1) * 256],
                                            in_=ptr[:]))
                tr_evac.append(c_vx.n)
            block.tensor(tr)
            block.vector(trev)
        vx_yfm = c_vx.n
        ptr_free = c_vx.n
        ybf_free = tr_done[-1]

        tn_op = []

        def oproj(tensor, pair=pair):
            c_vx.wait(tensor, vx_yfm)
            if pair == 0:
                c_lw.wait(tensor, lw_a1)
            for tt in range(2):
                for half in range(2):
                    bank = tt * 2 + half
                    c_sx.wait(tensor, pav_free[bank])
                    for dc in range(DC):
                        tensor.matmul(
                            out=pav[bank][:],
                            lhsT=yfm[:, dc * 256 + tt * 128: dc * 256 + tt * 128 + 128],
                            rhs=Wo_sb[:, dc * 1024 + half * 512: dc * 1024 + (half + 1) * 512],
                            start=(dc == 0), stop=False, skip_group_check=True)
                    ins = tensor.matmul(
                        out=pav[bank][:], lhsT=ones_row_b[:],
                        rhs=bor[:, half * 512:(half + 1) * 512],
                        start=False, stop=True, skip_group_check=True)
                    c_tn.inc(ins)
                    tn_op.append((c_tn.n, bank, tt, half))
        block.tensor(oproj)

        sx_orel = []
        res_done = []
        for i in range(4):
            def s_one(scalar, i=i):
                tnv, bank, tt, half = tn_op[i]
                c_tn.wait(scalar, tnv)
                if i >= 1:
                    c_vx.wait(scalar, res_done[i - 1])
                else:
                    c_vx.wait(scalar, orel_free)
                c_sx.inc(scalar.activation(out=orel[:], in_=pav[bank][:], func=AF.Relu))
                sx_orel.append((c_sx.n, tt, half))
                pav_free[bank] = c_sx.n

            def v_one(vector, i=i, pair=pair, ld_q=ld_q):
                sxv, tt, half = sx_orel[i]
                if i == 0:
                    s_qp.wait(vector, ld_q)
                c_sx.wait(vector, sxv)
                g = 2 * pair + tt
                c_vx.inc(vector.tensor_tensor(
                    out=resid[:, g * 1024 + half * 512: g * 1024 + (half + 1) * 512],
                    in0=xo_sb[:, tt * 1024 + half * 512: tt * 1024 + (half + 1) * 512],
                    in1=orel[:], op=ALU.add))
                res_done.append(c_vx.n)
            block.scalar(s_one)
            block.vector(v_one)
        orel_free = c_vx.n
        xo_free = c_vx.n
        qp_free = qp_free_next

    ctxA1.close()
    barrier()

    s_w01 = [Ctr(ctx, nc, f"s_w01{i}") for i in range(2)]
    s_w2 = [Ctr(ctx, nc, f"s_w2{i}") for i in range(2)]
    dma_ctrs += s_w01 + s_w2
    ffn_state = {"w01_round": 0, "w2_round": 0}


    def emit_ln_fm(tag, x_ln2, sq, xlt, s1c, s2c, meanc, tc, msqc, varc, stdc,
                   rstdc, ptrB):
        xlt_free = [0, 0]
        ptr_free_B = [0]
        for tt in range(NT):
            def ln_s(scalar, tt=tt):
                c_sx.inc(scalar.activation(out=sq[:], in_=resid[:, tt * 1024:(tt + 1) * 1024],
                                           func=AF.Square))
            def ln_v1(vector, tt=tt):
                c_vx.inc(vector.reduce_sum(out=s1c[:], in_=resid[:, tt * 1024:(tt + 1) * 1024],
                                           axis=AX.X))
            block.scalar(ln_s)
            sx_sq2 = c_sx.n
            block.vector(ln_v1)
            def ln_v2(vector, sx_sq2=sx_sq2):
                c_sx.wait(vector, sx_sq2)
                c_vx.inc(vector.reduce_sum(out=s2c[:], in_=sq[:], axis=AX.X))
            block.vector(ln_v2)
            vx_s12 = c_vx.n
            def ln_s2(scalar, vx_s12=vx_s12):
                c_vx.wait(scalar, vx_s12)
                c_sx.inc(scalar.mul(out=meanc[:], in_=s1c[:], mul=1.0 / D))
                scalar.mul(out=tc[:], in_=s2c[:], mul=1.0 / D)
                c_sx.wait(scalar)
                c_sx.inc(scalar.activation(out=msqc[:], in_=meanc[:], func=AF.Square))
            block.scalar(ln_s2)
            sx_ms = c_sx.n
            def ln_v3(vector, sx_ms=sx_ms):
                c_sx.wait(vector, sx_ms)
                c_vx.inc(vector.tensor_tensor(out=varc[:], in0=tc[:], in1=msqc[:],
                                              op=ALU.subtract))
            block.vector(ln_v3)
            vx_var2 = c_vx.n
            def ln_s3(scalar, vx_var2=vx_var2):
                c_vx.wait(scalar, vx_var2)
                c_sx.inc(scalar.activation(out=stdc[:], in_=varc[:], func=AF.Sqrt,
                                           bias=epsc[:]))
            block.scalar(ln_s3)
            sx_std2 = c_sx.n
            def ln_v4(vector, tt=tt, sx_std2=sx_std2):
                c_sx.wait(vector, sx_std2)
                c_vx.inc(vector.reciprocal(out=rstdc[:], in_=stdc[:]))
                c_vx.wait(vector)
                c_tn.wait(vector, xlt_free[tt % 2])
                c_vx.inc(vector.tensor_scalar(
                    out=xlt[tt % 2][:], in0=resid[:, tt * 1024:(tt + 1) * 1024],
                    scalar1=meanc[:], scalar2=rstdc[:],
                    op0=ALU.subtract, op1=ALU.mult))
            block.vector(ln_v4)
            vx_xlt = c_vx.n
            for b4 in range(2):
                def tr_b(tensor, tt=tt, b4=b4, vx_xlt=vx_xlt):
                    c_vx.wait(tensor, vx_xlt)
                    c_vx.wait(tensor, ptr_free_B[0])
                    c_ld.wait(tensor)
                    ins = None
                    for j in range(4):
                        dc = b4 * 4 + j
                        ins = tensor.transpose(
                            out=ptrB[:, j * 128:(j + 1) * 128],
                            in_=xlt[tt % 2][:, dc * 128:(dc + 1) * 128],
                            identity=ident[:])
                    c_tn.inc(ins)
                block.tensor(tr_b)
                tn_tr = c_tn.n
                def trev_b(vector, tt=tt, b4=b4, tn_tr=tn_tr):
                    c_tn.wait(vector, tn_tr)
                    dst = x_ln2.rearrange("p (dc n) -> p dc n", dc=DC)
                    c_vx.inc(vector.tensor_copy(
                        out=dst[:, b4 * 4:(b4 + 1) * 4, tt * 128:(tt + 1) * 128],
                        in_=ptrB.rearrange("p (j n) -> p j n", j=4)))
                    ptr_free_B[0] = c_vx.n
                block.vector(trev_b)
            xlt_free[tt % 2] = c_tn.n
        return c_vx.n, c_sx.n

    def emit_ffn(tag):
        ctxF = ExitStack()
        EF = ctxF.enter_context
        x_ln2 = EF(nc.sbuf_tensor(f"xln2{tag}", [128, DC * 2048], BF16))
        c_all = EF(nc.sbuf_tensor(f"call{tag}", [128, NT * 8], F32))

        ctxG = ExitStack()
        EG = ctxG.enter_context
        sq = EG(nc.sbuf_tensor(f"sq{tag}", [128, 1024], F32))
        xlt = [EG(nc.sbuf_tensor(f"xlt{tag}{b}", [128, 1024], BF16)) for b in range(2)]
        s1c = EG(nc.sbuf_tensor(f"s1c{tag}", [128, 1], F32))
        s2c = EG(nc.sbuf_tensor(f"s2c{tag}", [128, 1], F32))
        meanc = EG(nc.sbuf_tensor(f"meanc{tag}", [128, 1], F32))
        tc = EG(nc.sbuf_tensor(f"tc{tag}", [128, 1], F32))
        msqc = EG(nc.sbuf_tensor(f"msqc{tag}", [128, 1], F32))
        varc = EG(nc.sbuf_tensor(f"varc{tag}", [128, 1], F32))
        stdc = EG(nc.sbuf_tensor(f"stdc{tag}", [128, 1], F32))
        rstdc = EG(nc.sbuf_tensor(f"rstdc{tag}", [128, 1], F32))
        Wg_sb = EG(nc.sbuf_tensor(f"Wg{tag}", [128, DC * 8], BF16))
        bgb = EG(nc.sbuf_tensor(f"bgb{tag}", [128, 8], F32))
        glog = EG(nc.sbuf_tensor(f"glog{tag}", [128, 8], F32))
        m1 = EG(nc.sbuf_tensor(f"m1{tag}", [128, 1], F32))
        negm1 = EG(nc.sbuf_tensor(f"negm1{tag}", [128, 1], F32))
        m2 = EG(nc.sbuf_tensor(f"m2{tag}", [128, 1], F32))
        em = EG(nc.sbuf_tensor(f"em{tag}", [128, 1], F32))
        Zc = EG(nc.sbuf_tensor(f"Zc{tag}", [128, 1], F32))
        rZ = EG(nc.sbuf_tensor(f"rZ{tag}", [128, 1], F32))
        eq = EG(nc.sbuf_tensor(f"eq{tag}", [128, 8], F32))
        gm = EG(nc.sbuf_tensor(f"gm{tag}", [128, 8], F32))
        ge = EG(nc.sbuf_tensor(f"ge{tag}", [128, 8], F32))
        e1 = EG(nc.sbuf_tensor(f"e1{tag}", [128, 8], F32))
        cbf = EG(nc.sbuf_tensor(f"cbf{tag}", [128, 8], BF16))
        cT_sb = EG(nc.sbuf_tensor(f"cT{tag}", [8, NT * 128], BF16))
        b2_sb = EG(nc.sbuf_tensor(f"b2{tag}", [8, D], BF16))
        pgate = EG(nc.psum_tensor(f"pgate{tag}", [128, 8], F32))
        pct = EG(nc.psum_tensor(f"pct{tag}", [8, 128], BF16))
        ptrB = EG(nc.psum_tensor(f"ptrB{tag}", [128, 512], BF16))
        pb = [EG(nc.psum_tensor(f"pb{tag}{i}", [128, 512], F32)) for i in range(2)]

        def g_loads(gpsimd):
            c_lw.wait(gpsimd)
            for kc in range(DC):
                c_lw.inc(gpsimd.dma_start(
                    out=Wg_sb[:, kc * 8:(kc + 1) * 8],
                    in_=Wg[kc * 128:(kc + 1) * 128, :]), 16)
            c_lw.inc(gpsimd.dma_start(out=b2_sb[:], in_=b2_in), 16)
        block.gpsimd(g_loads)
        lw_g = c_lw.n

        def g_loads2(sync):
            c_ld.wait(sync)
            c_ld.inc(sync.dma_start(out=bgb[:], in_=bg_bc), 16)
        block.sync(g_loads2)
        ld_g = c_ld.n

        vx_ln_all, sx_ln_all = emit_ln_fm(tag, x_ln2, sq, xlt,
            s1c, s2c, meanc, tc, msqc, varc, stdc, rstdc, ptrB)

        for tt in range(NT):
            def gt_mm(tensor, tt=tt):
                if tt == 0:
                    c_lw.wait(tensor, lw_g)
                    c_vx.wait(tensor, vx_ln_all)
                for kc in range(DC):
                    ins = tensor.matmul(
                        out=pgate[:], lhsT=x_ln2[:, kc * 2048 + tt * 128: kc * 2048 + tt * 128 + 128],
                        rhs=Wg_sb[:, kc * 8:(kc + 1) * 8],
                        start=(kc == 0), stop=(kc == DC - 1), skip_group_check=True)
                c_tn.inc(ins)
            block.tensor(gt_mm)
            tn_g = c_tn.n

            def gt_v(vector, tt=tt, tn_g=tn_g):
                c_tn.wait(vector, tn_g)
                if tt == 0:
                    c_ld.wait(vector)
                c_vx.inc(vector.tensor_tensor(out=glog[:], in0=pgate[:], in1=bgb[:],
                                              op=ALU.add))
                c_vx.wait(vector)
                c_vx.inc(vector.reduce_max(out=m1[:], in_=glog[:], axis=AX.X))
                c_vx.inc(vector.reduce_max(out=negm1[:], in_=glog[:], axis=AX.X, negate=True))
                c_vx.wait(vector)
                c_vx.inc(vector.tensor_scalar(out=eq[:], in0=glog[:], scalar1=m1[:],
                                              scalar2=None, op0=ALU.is_equal))
                c_vx.wait(vector)
                c_vx.inc(vector.scalar_tensor_tensor(out=gm[:], in0=eq[:], scalar=-1e30,
                                                     in1=glog[:], op0=ALU.mult, op1=ALU.add))
                c_vx.wait(vector)
                c_vx.inc(vector.reduce_max(out=m2[:], in_=gm[:], axis=AX.X))
            block.vector(gt_v)
            vx_g1 = c_vx.n

            def gt_s(scalar, tt=tt, vx_g1=vx_g1):
                c_vx.wait(scalar, vx_g1)
                c_sx.inc(scalar.activation(out=em[:], in_=m2[:], func=AF.Exp, bias=negm1[:]))
                c_sx.wait(scalar)
                c_sx.inc(scalar.add(out=Zc[:], in_=em[:], add=1.0))
                c_sx.inc(scalar.activation(out=e1[:], in_=glog[:], func=AF.Exp, bias=negm1[:]))
            block.scalar(gt_s)
            sx_g1 = c_sx.n

            def gt_v2(vector, tt=tt, sx_g1=sx_g1):
                c_sx.wait(vector, sx_g1)
                c_vx.inc(vector.reciprocal(out=rZ[:], in_=Zc[:]))
                c_vx.inc(vector.tensor_scalar(out=ge[:], in0=glog[:], scalar1=m2[:],
                                              scalar2=None, op0=ALU.is_ge))
                c_vx.wait(vector)
                c_vx.inc(vector.scalar_tensor_tensor(
                    out=c_all[:, tt * 8:(tt + 1) * 8], in0=e1[:], scalar=rZ[:],
                    in1=ge[:], op0=ALU.mult, op1=ALU.mult))
                c_vx.wait(vector)
                c_vx.inc(vector.tensor_copy(out=cbf[:], in_=c_all[:, tt * 8:(tt + 1) * 8]))
            block.vector(gt_v2)
            vx_cbf = c_vx.n

            def gt_tr(tensor, tt=tt, vx_cbf=vx_cbf):
                c_vx.wait(tensor, vx_cbf)
                c_tn.inc(tensor.transpose(out=pct[:], in_=cbf[:], identity=ident[:]))
            block.tensor(gt_tr)
            tn_ct = c_tn.n

            def gt_v3(vector, tt=tt, tn_ct=tn_ct):
                c_tn.wait(vector, tn_ct)
                c_vx.inc(vector.tensor_copy(out=cT_sb[:, tt * 128:(tt + 1) * 128], in_=pct[:]))
            block.vector(gt_v3)
        vx_gate_all = c_vx.n

        tn_b2 = []
        pb_free = [0, 0]
        for tt in range(NT):
            def b2_mm(tensor, tt=tt):
                if tt == 0:
                    c_vx.wait(tensor, vx_gate_all)
                    c_sx.wait(tensor, sx_ln_all)
                for half in range(2):
                    c_vx.wait(tensor, pb_free[half])
                    c_tn.inc(tensor.matmul(
                        out=pb[half][:], lhsT=cT_sb[:, tt * 128:(tt + 1) * 128],
                        rhs=b2_sb[:, half * 512:(half + 1) * 512],
                        start=True, stop=True, skip_group_check=True))
                    tn_b2.append(c_tn.n)
            block.tensor(b2_mm)

            def b2_v(vector, tt=tt):
                for half in range(2):
                    c_tn.wait(vector, tn_b2[tt * 2 + half])
                    sl = resid[:, tt * 1024 + half * 512: tt * 1024 + (half + 1) * 512]
                    c_vx.inc(vector.tensor_tensor(out=sl, in0=sl, in1=pb[half][:],
                                                  op=ALU.add))
                    pb_free[half] = c_vx.n
            block.vector(b2_v)

        vx_g_end = c_vx.n
        sx_g_end = c_sx.n
        tn_g_end = c_tn.n
        ctxG.close()

        ctxE = ExitStack()
        EE = ctxE.enter_context
        hg = EE(nc.sbuf_tensor(f"hg{tag}", [128, 16 * 2048], BF16))
        w0s = [EE(nc.sbuf_tensor(f"w0s{tag}{b}", [128, DC * 128], BF16)) for b in range(2)]
        w1s = [EE(nc.sbuf_tensor(f"w1s{tag}{b}", [128, DC * 128], BF16)) for b in range(2)]
        w2s = [EE(nc.sbuf_tensor(f"w2s{tag}{b}", [128, 16 * 512], BF16)) for b in range(2)]
        gsil = [EE(nc.sbuf_tensor(f"gsil{tag}{b}", [128, 512], BF16)) for b in range(2)]
        b0p = EE(nc.sbuf_tensor(f"b0p{tag}", [128, 256], F32))
        b1p = EE(nc.sbuf_tensor(f"b1p{tag}", [128, 256], F32))
        ph = [EE(nc.psum_tensor(f"ph{tag}{i}", [128, 512], F32)) for i in range(2)]
        pg2 = [EE(nc.psum_tensor(f"pg2{tag}{i}", [128, 512], F32)) for i in range(2)]
        py = [EE(nc.psum_tensor(f"py{tag}{i}", [128, 512], F32)) for i in range(2)]

        def e_loads(sync):
            c_ld.wait(sync)
            c_vx.wait(sync, vx_g_end)
            c_sx.wait(sync, sx_g_end)
            c_tn.wait(sync, tn_g_end)
            c_ld.inc(sync.dma_start(out=b0p[:], in_=b0_pt), 16)
            c_ld.inc(sync.dma_start(out=b1p[:], in_=b1_pt), 16)
        block.sync(e_loads)
        ld_e = c_ld.n

        w01_free = [tn_g_end, tn_g_end]
        w2_free = [tn_g_end, tn_g_end]
        ph_free = [vx_g_end, vx_g_end]
        pg_free = [sx_g_end, sx_g_end]
        gs_free = [vx_g_end, vx_g_end]
        py_free = [vx_g_end, vx_g_end]
        hg_free = tn_g_end
        w2i = 0

        for e in range(NEXP):
            for hh in range(2):
                vx_hg0 = c_vx.n
                for ht16 in range(16):
                    ht = hh * 16 + ht16
                    wb = ht16 % 2

                    def w_load(gpsimd, e=e, ht=ht, wb=wb):
                        c_tn.wait(gpsimd, w01_free[wb])
                        src0 = W0[e][:, ht * 128:(ht + 1) * 128]
                        src0 = src0.rearrange("(kc p) n -> p kc n", p=128)
                        s_w01[wb].inc(gpsimd.dma_start(
                            out=w0s[wb].rearrange("p (kc n) -> p kc n", kc=DC), in_=src0), 16)
                        src1 = W1[e][:, ht * 128:(ht + 1) * 128]
                        src1 = src1.rearrange("(kc p) n -> p kc n", p=128)
                        s_w01[wb].inc(gpsimd.dma_start(
                            out=w1s[wb].rearrange("p (kc n) -> p kc n", kc=DC), in_=src1), 16)
                    block.gpsimd(w_load)
                    ld_w = s_w01[wb].n

                    for tq in range(4):
                        qb = tq % 2

                        def hgmm(tensor, e=e, ht16=ht16, tq=tq, wb=wb, qb=qb, ld_w=ld_w):
                            if tq == 0:
                                s_w01[wb].wait(tensor, ld_w)
                                if ht16 == 0:
                                    c_tn.wait(tensor, 0)
                            c_vx.wait(tensor, ph_free[qb])
                            for kc in range(DC):
                                ins = tensor.matmul(
                                    out=ph[qb][:],
                                    lhsT=w0s[wb][:, kc * 128:(kc + 1) * 128],
                                    rhs=x_ln2[:, kc * 2048 + tq * 512: kc * 2048 + (tq + 1) * 512],
                                    start=(kc == 0), stop=(kc == DC - 1), skip_group_check=True)
                            c_tn.inc(ins)
                            c_sx.wait(tensor, pg_free[qb])
                            for kc in range(DC):
                                ins = tensor.matmul(
                                    out=pg2[qb][:],
                                    lhsT=w1s[wb][:, kc * 128:(kc + 1) * 128],
                                    rhs=x_ln2[:, kc * 2048 + tq * 512: kc * 2048 + (tq + 1) * 512],
                                    start=(kc == 0), stop=(kc == DC - 1), skip_group_check=True)
                            c_tn.inc(ins)
                        block.tensor(hgmm)
                        tn_hg = c_tn.n
                        if tq == 3:
                            w01_free[wb] = c_tn.n

                        def sil(scalar, e=e, ht=ht, qb=qb, tn_hg=tn_hg):
                            c_tn.wait(scalar, tn_hg)
                            if e == 0 and ht == 0:
                                c_ld.wait(scalar, ld_e)
                            c_vx.wait(scalar, gs_free[qb])
                            c_sx.inc(scalar.activation(
                                out=gsil[qb][:], in_=pg2[qb][:], func=AF.Silu,
                                bias=b1p[:, e * 32 + ht: e * 32 + ht + 1]))
                            pg_free[qb] = c_sx.n
                        block.scalar(sil)
                        sx_sil = c_sx.n

                        def hgv(vector, e=e, ht=ht, ht16=ht16, tq=tq, qb=qb, sx_sil=sx_sil):
                            c_sx.wait(vector, sx_sil)
                            if e == 0 and ht == 0 and tq == 0:
                                c_ld.wait(vector, ld_e)
                            if ht16 == 0 and tq == 0:
                                c_tn.wait(vector, hg_free)
                            c_vx.inc(vector.scalar_tensor_tensor(
                                out=hg[:, ht16 * 2048 + tq * 512: ht16 * 2048 + (tq + 1) * 512],
                                in0=ph[qb][:], scalar=b0p[:, e * 32 + ht: e * 32 + ht + 1],
                                in1=gsil[qb][:], op0=ALU.add, op1=ALU.mult))
                            ph_free[qb] = c_vx.n
                            gs_free[qb] = c_vx.n
                        block.vector(hgv)
                vx_hg_all = c_vx.n

                for dmh in range(2):
                    w2b = w2i % 2

                    def w2_load(gpsimd, e=e, hh=hh, dmh=dmh, w2b=w2b):
                        c_tn.wait(gpsimd, w2_free[w2b])
                        src = W2[e][hh * 2048:(hh + 1) * 2048, dmh * 512:(dmh + 1) * 512]
                        src = src.rearrange("(ht p) n -> p ht n", p=128)
                        s_w2[w2b].inc(gpsimd.dma_start(
                            out=w2s[w2b].rearrange("p (ht n) -> p ht n", ht=16), in_=src), 16)
                    block.gpsimd(w2_load)
                    ld_w2 = s_w2[w2b].n
                    w2i += 1

                    for tt in range(NT):
                        yb = tt % 2

                        def ymm(tensor, tt=tt, w2b=w2b, yb=yb, ld_w2=ld_w2,
                                vx_hg_all=vx_hg_all):
                            if tt == 0:
                                s_w2[w2b].wait(tensor, ld_w2)
                                c_vx.wait(tensor, vx_hg_all)
                            c_vx.wait(tensor, py_free[yb])
                            for ht16 in range(16):
                                ins = tensor.matmul(
                                    out=py[yb][:],
                                    lhsT=hg[:, ht16 * 2048 + tt * 128: ht16 * 2048 + tt * 128 + 128],
                                    rhs=w2s[w2b][:, ht16 * 512:(ht16 + 1) * 512],
                                    start=(ht16 == 0), stop=(ht16 == 15), skip_group_check=True)
                            c_tn.inc(ins)
                        block.tensor(ymm)
                        tn_y = c_tn.n
                        if tt == NT - 1:
                            w2_free[w2b] = c_tn.n
                            if dmh == 1:
                                hg_free = c_tn.n

                        def yv(vector, e=e, tt=tt, dmh=dmh, yb=yb, tn_y=tn_y):
                            c_tn.wait(vector, tn_y)
                            sl = resid[:, tt * 1024 + dmh * 512: tt * 1024 + (dmh + 1) * 512]
                            c_vx.inc(vector.scalar_tensor_tensor(
                                out=sl, in0=py[yb][:],
                                scalar=c_all[:, tt * 8 + e: tt * 8 + e + 1],
                                in1=sl, op0=ALU.mult, op1=ALU.add))
                            py_free[yb] = c_vx.n
                        block.vector(yv)

        ctxE.close()
        ctxF.close()


    def emit_attn_b(level=3):
        cc_in_k = nc.dram_tensor("cc_in_k", [1024, 128], BF16)
        cc_out_k = nc.dram_tensor("cc_out_k", [4096, 128], BF16)
        cc_in_v = nc.dram_tensor("cc_in_v", [128, 1040], BF16)
        cc_out_v = nc.dram_tensor("cc_out_v", [512, 1040], BF16)

        ctxC = ExitStack()
        EC = ctxC.enter_context
        q_sb = EC(nc.sbuf_tensor("q_sb", [128, DC * 2048], BF16))
        k_sb = EC(nc.sbuf_tensor("k_sb", [128, DC * 2048], BF16))
        vaug = EC(nc.sbuf_tensor("vaug", [128, NT * 1040], BF16))
        inv8 = EC(nc.sbuf_tensor("inv8", [128, 1], F32))

        def initc(vector):
            c_vx.inc(vector.memset(inv8[:], 0.125))
            for tile in range(NT):
                va = vaug[:, tile * 1040:(tile + 1) * 1040]
                va = va.rearrange("p (h w) -> p h w", h=16)
                c_vx.inc(vector.memset(va[:, :, 64:65], 1.0))
        block.vector(initc)
        vx_init = [c_vx.n]

        ctxQ = ExitStack()
        EQ = ctxQ.enter_context
        x_ln3 = EQ(nc.sbuf_tensor("xln3", [128, DC * 2048], BF16))
        ctxLN = ExitStack()
        EL = ctxLN.enter_context
        sq3 = EL(nc.sbuf_tensor("sq3", [128, 1024], F32))
        xlt3 = [EL(nc.sbuf_tensor(f"xlt3{b}", [128, 1024], BF16)) for b in range(2)]
        cols3 = [EL(nc.sbuf_tensor(f"c3{i}", [128, 1], F32)) for i in range(8)]
        ptrC = EL(nc.psum_tensor("ptrC", [128, 512], BF16))
        vx_ln3, sx_ln3 = emit_ln_fm("C", x_ln3, sq3, xlt3, *cols3, ptrC)
        ctxLN.close()
        wp_free[0] = c_tn.n

        wpart = EQ(nc.sbuf_tensor("wpart", [128, DC * 512], BF16))
        bqkp = EQ(nc.sbuf_tensor("bqkp", [128, 16], F32))
        bv2r = EQ(nc.sbuf_tensor("bv2r", [1, D], BF16))
        pqk = [EQ(nc.psum_tensor(f"pqk{i}", [128, 512], F32)) for i in range(2)]

        def c_bias(sync):
            c_ld.wait(sync)
            c_vx.wait(sync, vx_ln3)
            c_sx.wait(sync, sx_ln3)
            c_tn.wait(sync, wp_free[0])
            c_ld.inc(sync.dma_start(out=bqkp[:], in_=battn_pt), 16)
        block.sync(c_bias)
        ld_cb = c_ld.n

        s_wp = Ctr(ctx, nc, "s_wp")
        dma_ctrs.append(s_wp)
        pqk_free = [0, 0]
        for part in range(2):
            dst = q_sb if part == 0 else k_sb
            for dsub in range(2):
                def wp_load(gpsimd, part=part, dsub=dsub):
                    c_tn.wait(gpsimd, wp_free[0])
                    if part == 0 and dsub == 0:
                        c_vx.wait(gpsimd, vx_ln3)
                        c_sx.wait(gpsimd, sx_ln3)
                    off = part * 1024 + dsub * 512
                    for kc in range(DC):
                        s_wp.inc(gpsimd.dma_start(
                            out=wpart[:, kc * 512:(kc + 1) * 512],
                            in_=Wattn[kc * 128:(kc + 1) * 128, off:off + 512]), 16)
                block.gpsimd(wp_load)
                ld_wp = s_wp.n

                for dt4 in range(4):
                    dt = dsub * 4 + dt4
                    for tg in range(4):
                        def qk_mm(tensor, part=part, dt4=dt4, dt=dt, tg=tg, ld_wp=ld_wp):
                            if dt4 == 0 and tg == 0:
                                s_wp.wait(tensor, ld_wp)
                                c_vx.wait(tensor, vx_ln3)
                            bank = (dt4 * 4 + tg) % 2
                            c_sx.wait(tensor, pqk_free[bank])
                            for kc in range(DC):
                                ins = tensor.matmul(
                                    out=pqk[bank][:],
                                    lhsT=wpart[:, kc * 512 + dt4 * 128: kc * 512 + dt4 * 128 + 128],
                                    rhs=x_ln3[:, kc * 2048 + tg * 512: kc * 2048 + (tg + 1) * 512],
                                    start=(kc == 0), stop=(kc == DC - 1), skip_group_check=True)
                            c_tn.inc(ins)
                        block.tensor(qk_mm)
                        tn_qk = c_tn.n

                        def qk_ev(scalar, part=part, dt4=dt4, dt=dt, tg=tg, tn_qk=tn_qk, dst=dst):
                            c_tn.wait(scalar, tn_qk)
                            if part == 0 and dt4 == 0 and tg == 0 and dt == 0:
                                c_ld.wait(scalar, ld_cb)
                            bank = (dt4 * 4 + tg) % 2
                            c_sx.inc(scalar.activation(
                                out=dst[:, dt * 2048 + tg * 512: dt * 2048 + (tg + 1) * 512],
                                in_=pqk[bank][:], func=AF.Identity,
                                bias=bqkp[:, part * 8 + dt: part * 8 + dt + 1]))
                            pqk_free[bank] = c_sx.n
                        block.scalar(qk_ev)
                wp_free[0] = c_tn.n

        vx_init_done = vx_init[0]
        for vh in range(2):
            def wv_load(gpsimd, vh=vh):
                c_tn.wait(gpsimd, wp_free[0])
                for kc in range(DC):
                    s_wp.inc(gpsimd.dma_start(
                        out=wpart[:, kc * 512:(kc + 1) * 512],
                        in_=Wattn[kc * 128:(kc + 1) * 128, 2048 + vh * 512: 2048 + (vh + 1) * 512]), 16)
                if vh == 0:
                    s_wp.inc(gpsimd.dma_start(out=bv2r[:], in_=battnv_row), 16)
            block.gpsimd(wv_load)
            ld_wv = s_wp.n

            for tile in range(NT):
                def v_mm(tensor, tile=tile, vh=vh, ld_wv=ld_wv):
                    if tile == 0:
                        s_wp.wait(tensor, ld_wv)
                    bank = tile % 2
                    c_sx.wait(tensor, pqk_free[bank])
                    for kc in range(DC):
                        tensor.matmul(
                            out=pqk[bank][:],
                            lhsT=x_ln3[:, kc * 2048 + tile * 128: kc * 2048 + tile * 128 + 128],
                            rhs=wpart[:, kc * 512:(kc + 1) * 512],
                            start=(kc == 0), stop=False, skip_group_check=True)
                    ins = tensor.matmul(
                        out=pqk[bank][:], lhsT=ones_row_b[:],
                        rhs=bv2r[:, vh * 512:(vh + 1) * 512],
                        start=False, stop=True, skip_group_check=True)
                    c_tn.inc(ins)
                block.tensor(v_mm)
                tn_v = c_tn.n

                def v_ev(scalar, tile=tile, vh=vh, tn_v=tn_v, vx_init_done=vx_init_done):
                    c_tn.wait(scalar, tn_v)
                    if tile == 0 and vh == 0:
                        c_vx.wait(scalar, vx_init_done)
                    bank = tile % 2
                    va = vaug[:, tile * 1040:(tile + 1) * 1040]
                    va = va.rearrange("p (h w) -> p h w", h=16)
                    c_sx.inc(scalar.activation(
                        out=va[:, vh * 8:(vh + 1) * 8, 0:64],
                        in_=pqk[bank].rearrange("p (h w) -> p h w", h=8),
                        func=AF.Copy))
                    pqk_free[bank] = c_sx.n
                block.scalar(v_ev)
            wp_free[0] = c_tn.n
        sx_qkv = c_sx.n

        ctxQ.close()
        if level <= 1:
            ctxC.close()
            return

        ctxX = ExitStack()
        EX = ctxX.enter_context
        kg = EX(nc.sbuf_tensor("kg", [128, DC * 512], BF16))
        vg = EX(nc.sbuf_tensor("vg", [128, 4 * 1040], BF16))
        wpj = EX(nc.sbuf_tensor("wpj", [128, DC * 1024], BF16))
        bpjr = EX(nc.sbuf_tensor("bpjr", [1, D], BF16))
        lcm = EX(nc.sbuf_tensor("lcm", [128, 512], F32))
        bmk = EX(nc.sbuf_tensor("bmk", [128, 64], F32))
        s_cc = Ctr(ctx, nc, "s_cc")
        s_cd = Ctr(ctx, nc, "s_cd")
        s_cg = Ctr(ctx, nc, "s_cg")
        dma_ctrs.extend([s_cc, s_cd, s_cg])

        def x_out(gpsimd):
            c_sx.wait(gpsimd, sx_qkv)
            for dt in range(DC):
                src = k_sb[:, dt * 2048:(dt + 1) * 2048]
                src = src.rearrange("p (b w) -> p b w", w=128)[:, :, 120:128]
                s_cc.inc(gpsimd.dma_start(
                    out=cc_in_k[dt * 128:(dt + 1) * 128, :].rearrange("r (b w) -> r b w", w=8),
                    in_=src), 16)
            for tile in range(NT):
                s_cc.inc(gpsimd.dma_start(
                    out=cc_in_v[tile * 8:(tile + 1) * 8, :],
                    in_=vaug[120:128, tile * 1040:(tile + 1) * 1040]), 16)
            s_cc.wait(gpsimd)
            gpsimd.collective_compute(
                "AllGather", ALU.bypass,
                replica_groups=[[0, 1, 2, 3], [4, 5, 6, 7]],
                ins=[cc_in_k.ap().opt()], outs=[cc_out_k.ap().opt()],
            ).then_inc(s_cg.sem)
            s_cg.n += 1
            gpsimd.collective_compute(
                "AllGather", ALU.bypass,
                replica_groups=[[0, 1, 2, 3], [4, 5, 6, 7]],
                ins=[cc_in_v.ap().opt()], outs=[cc_out_v.ap().opt()],
            ).then_inc(s_cg.sem)
            s_cg.n += 1
        block.gpsimd(x_out)
        cg_done = s_cg.n

        def g_in(sync):
            s_cg.wait(sync, cg_done)
            for r in range(4):
                for dt in range(DC):
                    s_cd.inc(sync.dma_start(
                        out=kg[:, dt * 512 + r * 128: dt * 512 + (r + 1) * 128],
                        in_=cc_out_k[r * 1024 + dt * 128: r * 1024 + (dt + 1) * 128, :]), 16)
                s_cd.inc(sync.dma_start(
                    out=vg[:, r * 1040:(r + 1) * 1040],
                    in_=cc_out_v[r * 128:(r + 1) * 128, :]), 16)
            c_ld.wait(sync)
            c_ld.inc(sync.dma_start(out=lcm[:], in_=lcmask4), 16)
            c_ld.inc(sync.dma_start(out=bmk[:], in_=bmask_in), 16)
        block.sync(g_in)
        ld_g2 = s_cd.n

        def pj_load(gpsimd):
            for dc in range(DC):
                s_cc.inc(gpsimd.dma_start(
                    out=wpj[:, dc * 1024:(dc + 1) * 1024],
                    in_=Wproj[dc * 128:(dc + 1) * 128, :]), 16)
            s_cc.inc(gpsimd.dma_start(out=bpjr[:], in_=bproj_row), 16)
        block.gpsimd(pj_load)
        ld_pj = s_cc.n
        if level <= 2:
            def drain2(sync):
                s_cd.wait(sync, ld_g2)
                s_cc.wait(sync, ld_pj)
            block.sync(drain2)
            ctxX.close()
            ctxC.close()
            return

        ao = EX(nc.sbuf_tensor("ao", [128, 1024], BF16))
        aofm = EX(nc.sbuf_tensor("aofm", [128, DC * 128], BF16))
        ploc_sb = EX(nc.sbuf_tensor("ploc_sb", [128, 1024], BF16))
        pstr_sb = EX(nc.sbuf_tensor("pstr_sb", [128, 1024], BF16))
        rl4 = EX(nc.sbuf_tensor("rl4", [128, 4], F32))
        psl = [EX(nc.psum_tensor(f"psl{i}", [128, 512], F32)) for i in range(2)]
        pss = [EX(nc.psum_tensor(f"pss{i}", [128, 512], F32)) for i in range(2)]
        pav2 = [EX(nc.psum_tensor(f"pavC{i}", [128, 260], F32)) for i in range(2)]
        ptrD = EX(nc.psum_tensor("ptrD", [128, 128], BF16))
        pop = [EX(nc.psum_tensor("pop0", [128, 512], F32))]

        def _pcol(hl):
            return (hl % 2) * 512 + (hl // 2) * 128

        psl_free = [0]
        pss_free = [0]
        pstr_free = [0]
        pav_free2 = [0, 0]
        ao_free = [0]
        aofm_free = [0]
        ptrD_free = [0]
        pop_free = [0]

        import os as _os
        _sub = _os.environ.get("KSUB", "LSAO")
        for slot in range(NT):
            for hp in range(2):
                def loc_mm(tensor, slot=slot, hp=hp):
                    if slot == 0 and hp == 0:
                        s_cc.wait(tensor, ld_pj)
                        s_cd.wait(tensor, ld_g2)
                        c_ld.wait(tensor)
                    c_sx.wait(tensor, psl_free[0])
                    ins = None
                    for hl in range(8):
                        h = hp * 8 + hl
                        dt, base = h // 2, (h % 2) * 64
                        bank = psl[hl % 2]
                        col = (hl // 2) * 128
                        ins = tensor.matmul(
                            out=bank[:, col:col + 128],
                            lhsT=k_sb[base:base + 64, dt * 2048 + slot * 128: dt * 2048 + slot * 128 + 128],
                            rhs=q_sb[base:base + 64, dt * 2048 + slot * 128: dt * 2048 + slot * 128 + 128],
                            start=(hl < 2), stop=(hl >= 6), skip_group_check=True)
                    c_tn.inc(ins)
                block.tensor(loc_mm)
                tn_loc = c_tn.n

                def loc_m(vector, tn_loc=tn_loc):
                    c_tn.wait(vector, tn_loc)
                    vector.tensor_tensor(out=psl[0][:], in0=psl[0][:],
                                         in1=lcm[:], op=ALU.add)
                    c_vx.inc(vector.tensor_tensor(out=psl[1][:], in0=psl[1][:],
                                                  in1=lcm[:], op=ALU.add))
                block.vector(loc_m)
                vx_lm = c_vx.n

                def loc_e(scalar, vx_lm=vx_lm):
                    c_vx.wait(scalar, vx_lm)
                    scalar.activation(out=ploc_sb[:, 0:512], in_=psl[0][:],
                                      func=AF.Exp, scale=inv8[:])
                    c_sx.inc(scalar.activation(out=ploc_sb[:, 512:1024], in_=psl[1][:],
                                               func=AF.Exp, scale=inv8[:]))
                    psl_free[0] = c_sx.n
                block.scalar(loc_e)
                sx_pl = c_sx.n

                do_av = "A" in _sub
                do_str = "S" in _sub
                if do_av:
                    def av_loc(tensor, slot=slot, hp=hp, sx_pl=sx_pl):
                        c_sx.wait(tensor, sx_pl)
                        c_vx.wait(tensor, max(pav_free2))
                        ins = None
                        for b in range(2):
                            for h4 in range(4):
                                hl = b * 4 + h4
                                h = hp * 8 + hl
                                pc = _pcol(hl)
                                va = vaug[:, slot * 1040 + h * 65: slot * 1040 + h * 65 + 65]
                                ins = tensor.matmul(
                                    out=pav2[b][:, h4 * 65:(h4 + 1) * 65],
                                    lhsT=ploc_sb[:, pc:pc + 128],
                                    rhs=va,
                                    start=(hl == 0 or hl == 4), stop=False,
                                    skip_group_check=True)
                        c_tn.inc(ins)
                    block.tensor(av_loc)

                for rt in range(4 if do_str else 0):
                    def str_mm(tensor, slot=slot, hp=hp, rt=rt):
                        c_sx.wait(tensor, pss_free[0])
                        ins = None
                        for hl in range(8):
                            h = hp * 8 + hl
                            dt, base = h // 2, (h % 2) * 64
                            bank = pss[hl % 2]
                            col = (hl // 2) * 128
                            ins = tensor.matmul(
                                out=bank[:, col:col + 128],
                                lhsT=kg[base:base + 64, dt * 512 + rt * 128: dt * 512 + (rt + 1) * 128],
                                rhs=q_sb[base:base + 64, dt * 2048 + slot * 128: dt * 2048 + slot * 128 + 128],
                                start=(hl < 2), stop=(hl >= 6), skip_group_check=True)
                        c_tn.inc(ins)
                    block.tensor(str_mm)
                    tn_str = c_tn.n

                    def str_e(scalar, slot=slot, rt=rt, tn_str=tn_str):
                        c_tn.wait(scalar, tn_str)
                        c_tn.wait(scalar, pstr_free[0])
                        scalar.activation(
                            out=pstr_sb[:, 0:512],
                            in_=pss[0][:], func=AF.Exp, scale=inv8[:],
                            bias=bmk[:, slot * 4 + rt: slot * 4 + rt + 1])
                        c_sx.inc(scalar.activation(
                            out=pstr_sb[:, 512:1024],
                            in_=pss[1][:], func=AF.Exp, scale=inv8[:],
                            bias=bmk[:, slot * 4 + rt: slot * 4 + rt + 1]))
                        pss_free[0] = c_sx.n
                    block.scalar(str_e)
                    sx_se = c_sx.n

                    if do_av:
                        def av_str(tensor, slot=slot, hp=hp, rt=rt, sx_se=sx_se):
                            c_sx.wait(tensor, sx_se)
                            ins = None
                            for b in range(2):
                                for h4 in range(4):
                                    hl = b * 4 + h4
                                    h = hp * 8 + hl
                                    pc = _pcol(hl)
                                    ins = tensor.matmul(
                                        out=pav2[b][:, h4 * 65:(h4 + 1) * 65],
                                        lhsT=pstr_sb[:, pc:pc + 128],
                                        rhs=vg[:, rt * 1040 + h * 65: rt * 1040 + h * 65 + 65],
                                        start=False, stop=(rt == 3 and hl in (3, 7)),
                                        skip_group_check=True)
                            c_tn.inc(ins)
                            pstr_free[0] = c_tn.n
                        block.tensor(av_str)

                if not do_av:
                    continue
                tn_av2 = c_tn.n

                def av_ev(vector, slot=slot, hp=hp, tn_av2=tn_av2):
                    c_tn.wait(vector, tn_av2)
                    if hp == 0:
                        c_tn.wait(vector, ao_free[0])
                    for b in range(2):
                        pv = pav2[b].rearrange("p (h w) -> p h w", h=4)
                        c_vx.inc(vector.reciprocal(out=rl4[:], in_=pv[:, :, 64]))
                        c_vx.wait(vector)
                        for h4 in range(4):
                            h = hp * 8 + b * 4 + h4
                            c_vx.inc(vector.tensor_scalar(
                                out=ao[:, h * 64:(h + 1) * 64],
                                in0=pav2[b][:, h4 * 65: h4 * 65 + 64],
                                scalar1=rl4[:, h4:h4 + 1], scalar2=None, op0=ALU.mult))
                        pav_free2[b] = c_vx.n
                block.vector(av_ev)
            vx_ao = c_vx.n

            if "O" not in _sub:
                continue
            for dc in range(DC):
                def ao_tr(tensor, slot=slot, dc=dc, vx_ao=vx_ao):
                    if dc == 0:
                        c_vx.wait(tensor, vx_ao)
                        c_tn.wait(tensor, aofm_free[0])
                    c_vx.wait(tensor, ptrD_free[0])
                    c_tn.inc(tensor.transpose(
                        out=ptrD[:], in_=ao[:, dc * 128:(dc + 1) * 128],
                        identity=ident[:]))
                block.tensor(ao_tr)
                tn_aot = c_tn.n

                def ao_ev(vector, dc=dc, tn_aot=tn_aot):
                    c_tn.wait(vector, tn_aot)
                    c_vx.inc(vector.tensor_copy(out=aofm[:, dc * 128:(dc + 1) * 128],
                                                in_=ptrD[:]))
                    ptrD_free[0] = c_vx.n
                block.vector(ao_ev)
            vx_aofm = c_vx.n
            ao_free[0] = c_tn.n

            for half in range(2):
                def pj_mm(tensor, slot=slot, half=half, vx_aofm=vx_aofm):
                    if half == 0:
                        c_vx.wait(tensor, vx_aofm)
                    bank = 0
                    c_vx.wait(tensor, pop_free[bank])
                    for dc in range(DC):
                        tensor.matmul(
                            out=pop[bank][:],
                            lhsT=aofm[:, dc * 128:(dc + 1) * 128],
                            rhs=wpj[:, dc * 1024 + half * 512: dc * 1024 + (half + 1) * 512],
                            start=(dc == 0), stop=False, skip_group_check=True)
                    ins = tensor.matmul(
                        out=pop[bank][:], lhsT=ones_row_b[:],
                        rhs=bpjr[:, half * 512:(half + 1) * 512],
                        start=False, stop=True, skip_group_check=True)
                    c_tn.inc(ins)
                block.tensor(pj_mm)
                tn_pj = c_tn.n
                if half == 1:
                    aofm_free[0] = c_tn.n

                def pj_add(vector, slot=slot, half=half, tn_pj=tn_pj):
                    c_tn.wait(vector, tn_pj)
                    bank = 0
                    sl = resid[:, slot * 1024 + half * 512: slot * 1024 + (half + 1) * 512]
                    c_vx.inc(vector.tensor_tensor(out=sl, in0=sl, in1=pop[bank][:],
                                                  op=ALU.add))
                    pop_free[bank] = c_vx.n
                block.vector(pj_add)

        ctxX.close()
        ctxC.close()

    wp_free = [0]

    import os
    _ph = os.environ.get("KPHASES", "ABCD")
    if "B" in _ph:
        emit_ffn("B")
        barrier()
    if "C" in _ph:
        emit_attn_b(level=int(os.environ.get("KLEVEL", "3")))
        barrier()
    if "D" in _ph:
        emit_ffn("D")
        barrier()

    def store_out(sync):
        c_vx.wait(sync)
        c_sx.wait(sync)
        c_tn.wait(sync)
        c_st.inc(sync.dma_start(
            out=out.rearrange("(t p) d -> p t d", p=128),
            in_=resid.rearrange("p (t d) -> p t d", t=NT)), 16)
        c_st.wait(sync)
    block.sync(store_out)

    ctx.close()
    return nc


def _bmask(core):
    seq = core // 4
    m = np.zeros((128, 64), dtype=np.float32)
    my_tiles = _assignment(core)
    for slot in range(NT):
        a_self = my_tiles[slot]
        for rt in range(4):
            rank_core = seq * 4 + rt
            r_tiles = _assignment(rank_core)
            for p in range(128):
                a_key = r_tiles[p // 8]
                if not (a_key < a_self):
                    m[p, slot * 4 + rt] = np.float32(-1e38)
    return np.ascontiguousarray(m)


def _host_inputs(core, inputs):
    import ml_dtypes
    x = np.asarray(inputs["L0_attn_a_x"])
    seq = core // 4
    tiles = _kv_perm_tiles(core)
    rows = np.concatenate([np.arange(128 * t, 128 * t + 128) for t in tiles])
    xp = x[seq][rows]
    xkv_cm = np.ascontiguousarray(xp.T)

    slots = _assignment(core)
    own_rows = np.concatenate([np.arange(128 * a, 128 * a + 128) for a in slots])
    xown = np.ascontiguousarray(x[seq][own_rows])

    k = core % 4
    am = np.zeros((CH, 1024, 256), dtype=np.float32)
    for c in range(CH):
        own = [8 * c + 2 * k, 8 * c + 2 * k + 1]
        rest = [8 * c + j for j in range(8) if 8 * c + j not in own]
        order = own + rest
        key_rows = np.concatenate([np.arange(128 * t, 128 * t + 128) for t in order])
        q_rows = np.concatenate([np.arange(128 * t, 128 * t + 128) for t in own])
        am[c] = np.where(key_rows[:, None] > q_rows[None, :], -10000.0, 0.0)

    return {
        "xkv_cm": xkv_cm,
        "xown": xown,
        "amask": am,
        "Wq": np.asarray(inputs["L0_attn_a_Wq"]),
        "Wk": np.asarray(inputs["L0_attn_a_Wk"]),
        "Wv": np.asarray(inputs["L0_attn_a_Wv"]),
        "Wo": np.asarray(inputs["L0_attn_a_Wo"]),
        "bq_pt": np.ascontiguousarray(np.asarray(inputs["L0_attn_a_bq"]).reshape(8, 128).T),
        "bk_pt": np.ascontiguousarray(np.asarray(inputs["L0_attn_a_bk"]).reshape(8, 128).T),
        "bv_row": np.ascontiguousarray(np.asarray(inputs["L0_attn_a_bv"]).reshape(1, D)),
        "bo_row": np.ascontiguousarray(np.asarray(inputs["L0_attn_a_bo"]).reshape(1, D)),
        "ident_bf": np.eye(128, dtype=ml_dtypes.bfloat16),
        "Wg": np.asarray(inputs["L0_ffn_Wg"]),
        "bg_bc": np.ascontiguousarray(np.broadcast_to(np.asarray(inputs["L0_ffn_bg"]), (128, 8))),
        "W0": np.asarray(inputs["L0_ffn_W0"]),
        "W1": np.asarray(inputs["L0_ffn_W1"]),
        "W2": np.asarray(inputs["L0_ffn_W2"]),
        "b0_pt": np.ascontiguousarray(np.asarray(inputs["L0_ffn_b0"]).reshape(8, 32, 128).transpose(2, 0, 1).reshape(128, 256)),
        "b1_pt": np.ascontiguousarray(np.asarray(inputs["L0_ffn_b1"]).reshape(8, 32, 128).transpose(2, 0, 1).reshape(128, 256)),
        "b2_in": np.asarray(inputs["L0_ffn_b2"]),
        "Wattn": np.asarray(inputs["L0_attn_b_W_attn"]),
        "battn_pt": np.ascontiguousarray(np.asarray(inputs["L0_attn_b_b_attn"])[0:2048].reshape(16, 128).T),
        "battnv_row": np.ascontiguousarray(np.asarray(inputs["L0_attn_b_b_attn"])[2048:3072].reshape(1, D)),
        "Wproj": np.asarray(inputs["L0_attn_b_W_proj"]),
        "bproj_row": np.ascontiguousarray(np.asarray(inputs["L0_attn_b_b_proj"]).reshape(1, D)),
        "lcmask4": np.ascontiguousarray(np.tile(
            np.where(np.arange(128)[:, None] > np.arange(128)[None, :],
                     np.float32(-1e38), np.float32(0.0)), (1, 4))),
        "bmask_in": _bmask(core),
    }


def kernel(**inputs):
    nc = bass.Bass("TRN2", target_bir_lowering=False,
                   dynamic_dma_scratch_size=16384)
    build(nc)
    in_maps = [_host_inputs(core, inputs) for core in range(NCORE)]
    res = run_bass_kernel_spmd(nc, in_maps, core_ids=list(range(NCORE)))

    full = np.zeros((B, S, D), dtype=np.float32)
    for core in range(NCORE):
        seq = core // 4
        o = np.asarray(res.results[core]["out"])
        for s, a in enumerate(_assignment(core)):
            full[seq, 128 * a:128 * a + 128] = o[128 * s:128 * s + 128]
    return full
